# version 09
# baseline (speedup 1.0000x reference)
import numpy as np
import ml_dtypes
from contextlib import ExitStack

import concourse.bass as bass
import concourse.mybir as mybir
from concourse.bass_utils import run_bass_kernel_spmd

F32 = mybir.dt.float32
BF16 = mybir.dt.bfloat16
AF = mybir.ActivationFunctionType
ALU = mybir.AluOpType
AX = mybir.AxisListType

D = 1024
H = 4096
NCORES = 8
EPS = 1e-5
SCALE = 1.0 / 32.0
BF = ml_dtypes.bfloat16


def _lo_block(w_s, stage, c):
    S = 1 << stage
    G = np.zeros((128, 128), dtype=np.float32)
    q = np.arange(128)
    q0 = q[(q & S) == 0]
    i0 = c * 128 + q0
    k = (i0 // (2 * S)) * S + (i0 % S)
    G[q0, q0] = w_s[k, 0, 0]
    G[q0, q0 + S] = w_s[k, 0, 1]
    G[q0 + S, q0] = w_s[k, 1, 0]
    G[q0 + S, q0 + S] = w_s[k, 1, 1]
    return G


def _hi_block(w_s, stage, c2):
    S = 1 << stage
    Sp = S // 32
    Hm = np.zeros((128, 128), dtype=np.float32)
    p2 = np.arange(128)
    i = p2 * 32 + c2
    p0 = p2[(i & S) == 0]
    i0 = p0 * 32 + c2
    k = (i0 // (2 * S)) * S + (i0 % S)
    Hm[p0, p0] = w_s[k, 0, 0]
    Hm[p0, p0 + Sp] = w_s[k, 0, 1]
    Hm[p0 + Sp, p0] = w_s[k, 1, 0]
    Hm[p0 + Sp, p0 + Sp] = w_s[k, 1, 1]
    return Hm


def _host_blobs(up_w, down_w):
    up_lo = np.stack([[_lo_block(up_w[s], s, c) for s in range(6, -1, -1)]
                      for c in range(32)])
    up_hi = np.stack([[_hi_block(up_w[s], s, c2) for s in range(11, 6, -1)]
                      for c2 in range(32)])
    dn_hi = np.stack([[_hi_block(down_w[s], s, c2).T for s in range(7, 12)]
                      for c2 in range(32)])
    dn_lo = np.stack([[_lo_block(down_w[s], s, c).T for s in range(0, 7)]
                      for c in range(32)])
    return (np.ascontiguousarray(up_lo, np.float32),
            np.ascontiguousarray(up_hi, np.float32),
            np.ascontiguousarray(dn_hi, np.float32),
            np.ascontiguousarray(dn_lo, np.float32))


def _col8(b):
    return np.ascontiguousarray(np.asarray(b, np.float32).reshape(8, 128).T)


ENGINES = ("tensor", "vector", "scalar", "gpsimd", "sync")


class Sem:
    def __init__(self, ctx, nc, name):
        self.h = ctx.enter_context(nc.semaphore(name))
        self.v = 0
        self.name = name


class Prog:
    def __init__(self):
        self.ops = {e: [] for e in ENGINES}
        self.sems = []
        self.fence_waits = []
        self.fenced = set()

    def fence(self, exclude=()):
        self.fence_waits = [(s, s.v) for s in self.sems
                            if s.v > 0 and s not in exclude]
        self.fenced = set()

    def op(self, engine, fn, waits=(), inc=None, incn=1, nlist=1):
        w = [x for x in waits if x is not None]
        key = (engine, inc.name if inc is not None else "_")
        if key not in self.fenced:
            self.fenced.add(key)
            w = list(self.fence_waits) + w
        entry = [tuple(w), fn, None]
        if inc is not None:
            inc.v += incn * nlist
            entry[2] = (inc, incn)
        self.ops[engine].append(entry)
        return inc.v if inc is not None else None

    def wait(self, engine, *waits):
        self.ops[engine].append([tuple(w for w in waits if w is not None), None, None])

    def replay(self, block):
        def mk(name):
            ops = self.ops[name]

            def body(eng):
                for waits, fn, inc in ops:
                    for (sem, val) in waits:
                        if val > 0:
                            eng.wait_ge(sem.h, val)
                    if fn is not None:
                        ins = fn(eng)
                        if inc is not None:
                            if not isinstance(ins, list):
                                ins = [ins]
                            for i in ins:
                                i.then_inc(inc[0].h, inc[1])
            return body

        for e in ENGINES:
            getattr(block, e)(mk(e))


class SlotGuard:

    def __init__(self):
        self.d = {}

    def use(self, key):
        return self.d.get(key)

    def set(self, key, sem, val):
        self.d[key] = (sem, val)


def build_nc(T=4096, stop_after=None):
    SEQ = 4 * T
    NT0 = T // 512
    NG = T // 512
    SLOT = 512
    SPAN = 2048
    NSLOT = SEQ // SLOT
    NSPAN = SEQ // SPAN
    SPS = SPAN // SLOT
    CPS = SPAN // 128
    NRING = min(5, NSLOT)
    NFT = T // 512
    TPC = T // 128

    nc = bass.Bass("TRN2", target_bir_lowering=False)

    def din(name, shape, dt=F32):
        return nc.dram_tensor(name, shape, dt, kind="ExternalInput")

    x_fm = din("x_fm", [D, T])
    wqT = din("wqT", [D, D])
    wkT = din("wkT", [D, D])
    wvT = din("wvT", [D, D])
    woT = din("woT", [D, D])
    bq8 = din("bq8", [128, 8])
    bk8 = din("bk8", [128, 8])
    bo8 = din("bo8", [128, 8])
    bv_row = din("bv_row", [1, D])
    id_bf = din("id_bf", [128, 128], BF16)
    idf = din("idf", [128, 128])
    wexpT_aug = din("wexpT_aug", [H, D + 1])
    wconH = din("wconH", [H, D])
    up_lo_b = din("up_lo_b", [32, 7, 128, 128])
    up_hi_b = din("up_hi_b", [32, 5, 128, 128])
    dn_hi_b = din("dn_hi_b", [32, 5, 128, 128])
    dn_lo_b = din("dn_lo_b", [32, 7, 128, 128])
    upb_hp = din("upb_hp", [128, 32])
    bc8 = din("bc8", [128, 8])
    bd8 = din("bd8", [128, 8])

    y_fm = nc.dram_tensor("y_fm", [D, T], F32, kind="ExternalOutput")

    qspill = nc.dram_tensor("qspill", [D, T], BF16, kind="Internal")
    CCR = 2 * D + 1
    kv_cc_in = nc.dram_tensor("kv_cc_in", [CCR, T], BF16, kind="Internal")
    kv_cc_out = nc.dram_tensor("kv_cc_out", [4 * CCR, T], BF16, kind="Internal")
    def vflat(ap, rank_base, tb, ntok):
        r0 = rank_base + D + (tb * D) // T
        nr = (ntok * D) // T
        a = ap[r0:r0 + nr, :]
        if T > D:
            return a.rearrange("a (b c) -> (a b) c", b=T // D)
        if T < D:
            return a.rearrange("(a b) c -> a (b c)", b=D // T)
        return a

    x1sp = nc.dram_tensor("x1sp", [D, T], F32, kind="Internal")
    wtmp_up = nc.dram_tensor("wtmp_up", [H, D + 1], F32, kind="Internal")
    wtmp_dn = nc.dram_tensor("wtmp_dn", [H, D], F32, kind="Internal")
    w1blob = nc.dram_tensor("w1blob", [32, 8, 128, 128], BF16, kind="Internal")
    w2blob = nc.dram_tensor("w2blob", [H, D], BF16, kind="Internal")

    ctx = ExitStack()
    with ctx:
        def sb(name, shape, dt=F32, st=None):
            return (st or ctx).enter_context(nc.sbuf_tensor(name, shape, dt))

        def psm(name, shape, dt=F32, st=None):
            return (st or ctx).enter_context(nc.psum_tensor(name, shape, dt))

        ones_bf = sb("ones_bf", [128, 1], BF16)
        ones_f1 = sb("ones_f1", [1, 128], F32)
        one_f = sb("one_f", [1, 1], F32)
        eps_t = sb("eps_t", [1, 1], F32)
        idbf_sb = sb("idbf_sb", [128, 128], BF16)
        idf_sb = sb("idf_sb", [128, 128], F32)
        bo_sb = sb("bo_sb", [128, 8], F32)
        ob_sb = sb("ob_sb", [128, 8], F32)
        bd_sb = sb("bd_sb", [128, 8], F32)
        vcol_sb = sb("vcol_sb", [128, 32], F32)
        expb_sb = sb("expb_sb", [128, 1], F32)
        kmax_sb = sb("kmax_sb", [1, 1], F32)
        qmax_sb = sb("qmax_sb", [1, 1], F32)
        row_sum = sb("row_sum", [1, 512], F32)
        row_sq = sb("row_sq", [1, 512], F32)
        row_m = sb("row_m", [1, 512], F32)
        row_r = sb("row_r", [1, 512], F32)
        row_mr = sb("row_mr", [1, 512], F32)
        row_t = sb("row_t", [1, 512], F32)
        bvh = [sb(f"bvh{i}", [1, 512], F32) for i in range(2)]
        tiny = sb("tiny", [1, 16], F32)

        P = Prog()

        def Sem_(c, n_, name):
            s = Sem(c, n_, name)
            P.sems.append(s)
            return s
        sCC, sPE, sDVE, sACT, sGP = (Sem_(ctx, nc, n) for n in ["sCC", "sPE", "sDVE", "sACT", "sGP"])
        s_cst, s_wc, s_qsp, s_ksp, s_vsp, s_kmi, s_kmo = (
            Sem_(ctx, nc, n) for n in
            ["s_cst", "s_wc", "s_qsp", "s_ksp", "s_vsp", "s_kmi", "s_kmo"])
        s_xt = [Sem_(ctx, nc, f"s_xt{i}") for i in range(2)]
        s_gb = [Sem_(ctx, nc, f"s_gb{i}") for i in range(4)]
        s_wr = [Sem_(ctx, nc, f"s_wr{i}") for i in range(4)]
        s_wtup = [Sem_(ctx, nc, f"s_wtup{i}") for i in range(4)]
        s_wtdn = [Sem_(ctx, nc, f"s_wtdn{i}") for i in range(4)]
        s_w2b = [Sem_(ctx, nc, f"s_w2b{i}") for i in range(4)]
        s_w1b = Sem_(ctx, nc, "s_w1b")
        s_ring = [Sem_(ctx, nc, f"s_ring{i}") for i in range(NRING)]
        s_qg, s_xres, s_x1s, s_x1t, s_yout = (
            Sem_(ctx, nc, n) for n in ["s_qg", "s_xres", "s_x1s", "s_x1t", "s_yout"])
        s_w1s = [Sem_(ctx, nc, f"s_w1s{i}") for i in range(4)]
        s_w2s = [Sem_(ctx, nc, f"s_w2s{i}") for i in range(4)]

        rowg = SlotGuard()

        def _stop_here():
            P.op("scalar", lambda s: s.dma_start(out=y_fm[:, :], in_=x_fm[:, :]),
                 inc=s_yout, incn=16)
            P.wait("scalar", (s_yout, s_yout.v))
            block = ctx.enter_context(nc.Block())
            P.replay(block)

        def ln_rows(stat_pe_val, n, p_sum, p_sq, p_mb, p_rb):
            P.op("vector", lambda v: v.tensor_copy(row_sum[:, 0:n], p_sum[0:1, 0:n]),
                 waits=[(sPE, stat_pe_val), rowg.use("rows_act")], inc=sDVE)
            P.op("vector", lambda v: v.tensor_copy(row_sq[:, 0:n], p_sq[0:1, 0:n]),
                 inc=sDVE)
            w_ev = sDVE.v
            P.op("vector", lambda v: v.tensor_scalar_mul(row_m[:, 0:n], row_sum[:, 0:n], 1.0 / D),
                 waits=[(sDVE, w_ev)], inc=sDVE)
            P.op("vector", lambda v: v.tensor_scalar_mul(row_t[:, 0:n], row_sq[:, 0:n], 1.0 / D),
                 inc=sDVE)
            w1 = sDVE.v
            P.op("vector", lambda v: v.tensor_mul(row_sq[:, 0:n], row_m[:, 0:n], row_m[:, 0:n]),
                 waits=[(sDVE, w1)], inc=sDVE)
            w2 = sDVE.v
            P.op("vector", lambda v: v.tensor_sub(row_t[:, 0:n], row_t[:, 0:n], row_sq[:, 0:n]),
                 waits=[(sDVE, w2)], inc=sDVE)
            wvar = sDVE.v
            P.op("scalar", lambda s: s.activation(row_r[:, 0:n], row_t[:, 0:n], AF.Ln, bias=eps_t[:]),
                 waits=[(sDVE, wvar)], inc=sACT)
            wl = sACT.v
            P.op("scalar", lambda s: s.activation(row_r[:, 0:n], row_r[:, 0:n], AF.Exp, scale=-0.5),
                 waits=[(sACT, wl)], inc=sACT)
            wr = sACT.v
            P.op("vector", lambda v: v.tensor_mul(row_m[:, 0:n], row_m[:, 0:n], row_r[:, 0:n]),
                 waits=[(sACT, wr)], inc=sDVE)
            wmr = sDVE.v
            P.op("tensor", lambda t: t.matmul(p_rb[:, 0:n], ones_f1[:], row_r[:, 0:n],
                                              start=True, stop=True),
                 waits=[(sACT, wr), (sDVE, wmr)], inc=sPE)
            P.op("tensor", lambda t: t.matmul(p_mb[:, 0:n], ones_f1[:], row_m[:, 0:n],
                                              start=True, stop=True), inc=sPE)
            rowg.set("rows_act", sACT, wr)
            return sPE.v

        p0 = ExitStack()
        p0p = ExitStack()
        x_t = [sb(f"x_t{i}", [128, 8 * 512], F32, p0) for i in range(2)]
        xbf_t2 = [sb(f"xbf_t{i}", [128, 8 * 512], BF16, p0) for i in range(2)]
        xsq_t = sb("xsq_t", [128, 512], BF16, p0)
        xn_t2 = [sb(f"xn_t{i}", [128, 8 * 512], BF16, p0) for i in range(2)]
        wq_sb = sb("wq_sb", [128, 8 * D], BF16, p0)
        wk_sb = sb("wk_sb", [128, 8 * D], BF16, p0)
        wv_sb = sb("wv_sb", [128, 8 * D], BF16, p0)
        bq_sb = sb("bq_sb", [128, 8], F32, p0)
        bk_sb = sb("bk_sb", [128, 8], F32, p0)
        bvb_sb = sb("bvb_sb", [128, D], F32, p0)
        qout = sb("qout", [128, 8 * 512], BF16, p0)
        vout = sb("vout", [128, 2 * 512], BF16, p0)
        nsq = sb("nsq", [128, 512], BF16, p0)
        kmz = sb("kmz", [1, 512], F32, p0)

        ps_sum = psm("ps_sum", [1, 512], F32, p0p)
        ps_sq = psm("ps_sq", [1, 512], F32, p0p)
        ps_mb = psm("ps_mb", [128, 512], F32, p0p)
        ps_rb = psm("ps_rb", [128, 512], F32, p0p)
        ps_p = [psm(f"ps_p{i}", [128, 512], F32, p0p) for i in range(2)]
        ps_n = psm("ps_n", [1, 512], F32, p0p)

        pguard = SlotGuard()
        bufg = SlotGuard()

        P.op("sync", lambda s: [
            s.dma_start(out=idbf_sb[:], in_=id_bf[:, :]),
            s.dma_start(out=idf_sb[:], in_=idf[:, :]),
            s.dma_start(out=bq_sb[:], in_=bq8[:, :]),
            s.dma_start(out=bk_sb[:], in_=bk8[:, :]),
            s.dma_start(out=bo_sb[:], in_=bo8[:, :]),
            s.dma_start(out=ob_sb[:], in_=bc8[:, :]),
            s.dma_start(out=bd_sb[:], in_=bd8[:, :]),
            s.dma_start(out=vcol_sb[:], in_=upb_hp[:, :]),
            s.dma_start(out=bvh[0][:], in_=bv_row[:, 0:512]),
            s.dma_start(out=bvh[1][:], in_=bv_row[:, 512:1024]),
        ], inc=s_cst, incn=16, nlist=10)
        w_consts = s_cst.v
        P.op("vector", lambda v: v.tensor_add(ob_sb[:], ob_sb[:], bd_sb[:]),
             waits=[(s_cst, w_consts)], inc=sDVE)
        w_ob = sDVE.v
        P.op("vector", lambda v: [
            v.memset(ones_bf[:], 1.0),
            v.memset(ones_f1[:], 1.0),
            v.memset(one_f[:], 1.0),
            v.memset(eps_t[:], EPS),
            v.memset(qmax_sb[:], 0.0),
            v.memset(kmax_sb[:], 0.0),
        ], inc=sDVE, incn=1, nlist=6)
        w_cdve = sDVE.v

        P.op("gpsimd", lambda g: [
            g.dma_start(out=wq_sb[:].rearrange("p (c e) -> p c e", c=8),
                        in_=wqT.ap().rearrange("(c p) e -> p c e", p=128)),
            g.dma_start(out=wk_sb[:].rearrange("p (c e) -> p c e", c=8),
                        in_=wkT.ap().rearrange("(c p) e -> p c e", p=128)),
            g.dma_start(out=wv_sb[:].rearrange("p (c e) -> p c e", c=8),
                        in_=wvT.ap().rearrange("(c p) e -> p c e", p=128)),
        ], inc=s_wc, incn=16, nlist=3)
        w_wcast = s_wc.v

        for h2 in range(2):
            P.op("tensor", lambda t, h2=h2: t.matmul(
                ps_p[h2][:, :], ones_f1[:], bvh[h2][:],
                start=True, stop=True),
                waits=[(s_cst, w_consts), (sDVE, w_cdve)], inc=sPE)
            wb = sPE.v
            P.op("vector", lambda v, h2=h2: v.tensor_copy(bvb_sb[:, h2 * 512:(h2 + 1) * 512], ps_p[h2][:, :]),
                 waits=[(sPE, wb)], inc=sDVE)
            pguard.set(h2, sDVE, sDVE.v)
        w_bvb = sDVE.v

        xload_w = {}
        xdone = {}
        for t in range(NT0):
            buf = t % 2
            xt = x_t[buf]
            xbf_t = xbf_t2[buf]
            xn_t = xn_t2[buf]
            P.op("sync", lambda s, t=t, xt=xt: s.dma_start(
                out=xt[:].rearrange("p (c f) -> p c f", c=8),
                in_=x_fm.ap().rearrange("(c p) f -> p c f", p=128)[:, :, t * 512:(t + 1) * 512]),
                waits=[bufg.use(("x", buf))], inc=s_xt[buf], incn=16)
            xload_w[t] = (s_xt[buf], s_xt[buf].v)

            P.op("vector", lambda v, xt=xt, xbf_t=xbf_t: v.tensor_copy(xbf_t[:], xt[:]),
                 waits=[xload_w[t], (sDVE, w_bvb), bufg.use(("xbf", buf))], inc=sDVE)
            w_cast = sDVE.v
            for c in range(8):
                P.op("scalar", lambda s, c=c, xt=xt: s.activation(
                    xsq_t[:], xt[:, c * 512:(c + 1) * 512], AF.Square),
                    waits=[xload_w[t], bufg.use(("xsq",))], inc=sACT)
                w_sq = sACT.v
                P.op("tensor", lambda t_, c=c, xbf_t=xbf_t: t_.matmul(
                    ps_sum[0:1, :], ones_bf[:], xbf_t[:, c * 512:(c + 1) * 512],
                    start=(c == 0), stop=(c == 7)),
                    waits=([(sDVE, w_cast), rowg.use("stat")] if c == 0 else []))
                P.op("tensor", lambda t_, c=c: t_.matmul(
                    ps_sq[0:1, :], ones_bf[:], xsq_t[:],
                    start=(c == 0), stop=(c == 7)),
                    waits=[(sACT, w_sq)], inc=sPE)
                bufg.set(("xsq",), sPE, sPE.v)
            w_stat = sPE.v
            bufg.set(("xbf", buf), sPE, w_stat)
            bc = ln_rows(w_stat, 512, ps_sum, ps_sq, ps_mb, ps_rb)
            rowg.set("stat", sDVE, sDVE.v - 5)
            for c in range(8):
                P.op("vector", lambda v, c=c, xt=xt: v.tensor_mul(
                    xt[:, c * 512:(c + 1) * 512], xt[:, c * 512:(c + 1) * 512], ps_rb[:, :]),
                    waits=([(sPE, bc), bufg.use(("xn", buf))] if c == 0 else []), inc=sDVE)
                wmu = sDVE.v
                P.op("vector", lambda v, c=c, xt=xt, xn_t=xn_t: v.tensor_sub(
                    xn_t[:, c * 512:(c + 1) * 512], xt[:, c * 512:(c + 1) * 512], ps_mb[:, :]),
                    waits=[(sDVE, wmu)], inc=sDVE)
            w_xn = sDVE.v
            bufg.set(("x", buf), sDVE, w_xn)

            for wi, (wsb, bias, outb, spill) in enumerate((
                    (wq_sb, bq_sb, qout, qspill),
                    (wk_sb, bk_sb, qout, None))):
                pmax = qmax_sb if wi == 0 else kmax_sb
                for es in range(8):
                    slot = es % 2
                    for dc in range(8):
                        P.op("tensor", lambda t_, wsb=wsb, es=es, dc=dc, slot=slot, xn_t=xn_t: t_.matmul(
                            ps_p[slot][:, :], wsb[:, dc * D + es * 128:dc * D + es * 128 + 128],
                            xn_t[:, dc * 512:(dc + 1) * 512],
                            start=(dc == 0), stop=(dc == 7)),
                            waits=([(sDVE, w_xn), (s_wc, w_wcast), pguard.use(slot)] if dc == 0 else []),
                            inc=(sPE if dc == 7 else None))
                    wmm = sPE.v
                    P.op("scalar", lambda s, es=es, slot=slot, outb=outb, bias=bias: s.activation(
                        outb[:, es * 512:(es + 1) * 512], ps_p[slot][:, :], AF.Identity,
                        bias=bias[:, es:es + 1]),
                        waits=[(sPE, wmm), bufg.use(("po", id(outb)))], inc=sACT)
                    P.op("scalar", lambda s, es=es, slot=slot, bias=bias: s.activation(
                        nsq[:], ps_p[slot][:, :], AF.Square, bias=bias[:, es:es + 1]),
                        waits=[bufg.use(("nsq",))], inc=sACT)
                    pguard.set(slot, sACT, sACT.v)
                    w_sqn = sACT.v
                    P.op("tensor", lambda t_, es=es: t_.matmul(
                        ps_n[0:1, :], ones_bf[:], nsq[:], start=(es == 0), stop=(es == 7)),
                        waits=[(sACT, w_sqn)] + ([rowg.use("psn")] if es == 0 else []),
                        inc=sPE)
                    bufg.set(("nsq",), sPE, sPE.v)
                w_nmm = sPE.v
                spill_ap = (qspill.ap() if spill is qspill else kv_cc_in.ap()[0:D, :])
                s_sp = s_qsp if spill is qspill else s_ksp
                P.op("gpsimd", lambda g, outb=outb, spill_ap=spill_ap, t=t: g.dma_start(
                    out=spill_ap.rearrange("(c p) f -> p c f", p=128)[:, :, t * 512:(t + 1) * 512],
                    in_=outb[:].rearrange("p (c f) -> p c f", c=8)),
                    waits=[(sACT, sACT.v)], inc=s_sp, incn=16)
                bufg.set(("po", id(outb)), s_sp, s_sp.v)
                P.op("vector", lambda v: v.reduce_max(tiny[:, 0:1], ps_n[0:1, :], axis=AX.X),
                     waits=[(sPE, w_nmm), rowg.use("tiny1")], inc=sDVE)
                wtm = sDVE.v
                rowg.set("psn", sDVE, wtm)
                P.op("vector", lambda v, pmax=pmax: v.tensor_max(pmax[:], pmax[:], tiny[:, 0:1]),
                     waits=[(sDVE, wtm)], inc=sDVE)
                rowg.set("tiny1", sDVE, sDVE.v)

            for ts_ in range(4):
                for eh in range(2):
                    slot = eh
                    for dc in range(8):
                        P.op("tensor", lambda t_, ts_=ts_, eh=eh, dc=dc, slot=slot, xn_t=xn_t: t_.matmul(
                            ps_p[slot][:, :], xn_t[:, dc * 512 + ts_ * 128:dc * 512 + ts_ * 128 + 128],
                            wv_sb[:, dc * D + eh * 512:dc * D + (eh + 1) * 512],
                            start=(dc == 0), stop=(dc == 7)),
                            waits=([pguard.use(slot)] if dc == 0 else []),
                            inc=(sPE if dc == 7 else None))
                    wmm = sPE.v
                    P.op("vector", lambda v, eh=eh, slot=slot: v.tensor_add(
                        vout[:, eh * 512:(eh + 1) * 512], ps_p[slot][:, :], bvb_sb[:, eh * 512:(eh + 1) * 512]),
                        waits=[(sPE, wmm), bufg.use(("vout",))], inc=sDVE)
                    pguard.set(slot, sDVE, sDVE.v)
                wvev = sDVE.v
                P.op("gpsimd", lambda g, ts_=ts_, t=t: g.dma_start(
                    out=vflat(kv_cc_in.ap(), 0, t * 512 + ts_ * 128, 128),
                    in_=vout[:]),
                    waits=[(sDVE, wvev)], inc=s_vsp, incn=16)
                bufg.set(("vout",), s_vsp, s_vsp.v)
            xdone[t] = sPE.v
            bufg.set(("xn", buf), sPE, xdone[t])
        kv_spill_w = None

        P.op("vector", lambda v: v.memset(kmz[:], 0.0), inc=sDVE)
        w_ms = sDVE.v
        P.op("vector", lambda v: v.tensor_copy(kmz[:, 0:1], kmax_sb[:]),
             waits=[(sDVE, w_ms)], inc=sDVE)
        w_km = sDVE.v
        P.op("gpsimd", lambda g: [
            g.dma_start(out=kv_cc_in[CCR - 1:CCR, j * 512:(j + 1) * 512], in_=kmz[:])
            for j in range(T // 512)],
            waits=[(sDVE, w_km)], inc=s_kmi, incn=16, nlist=T // 512)
        w_kmdma = s_kmi.v
        P.op("gpsimd", lambda g: g.collective_compute(
            "AllGather", ALU.bypass,
            replica_groups=[[0, 1, 2, 3], [4, 5, 6, 7]],
            ins=[kv_cc_in.ap().opt()],
            outs=[kv_cc_out.ap().opt()]),
            waits=[(s_ksp, s_ksp.v), (s_vsp, s_vsp.v), (s_kmi, w_kmdma)], inc=sCC)
        w_cc = sCC.v
        P.op("gpsimd", lambda g: [
            g.dma_start(out=tiny[:, r:r + 1], in_=kv_cc_out[(r + 1) * CCR - 1:(r + 1) * CCR, 0:1])
            for r in range(4)],
            waits=[(sCC, w_cc), (sDVE, w_km)], inc=s_kmo, incn=16, nlist=4)
        w_kmrb = s_kmo.v

        p0p.close()
        w_p0_dve = sDVE.v
        w_p0_act = sACT.v
        if stop_after == "p0":
            p0.close()
            P.fence(exclude=(sCC, s_kmo))
            _stop_here()
            return nc

        pp = ExitStack()
        NW = 4
        NCH = 3
        gblob = [sb(f"gblob{i}", [128, 7 * 128], F32, pp) for i in range(NW)]
        t_pp = [sb(f"t_pp{i}", [128, 128], F32, pp) for i in range(2 * NW)]
        wrow = [sb(f"wrow{i}", [128, D + 1], F32, pp) for i in range(NW)]
        wev = [sb(f"wev{i}", [128, D + 1], F32, pp) for i in range(NW)]
        wbf = [sb(f"wbf{i}", [128, D], BF16, pp) for i in range(2)]
        wtr = sb("wtr", [128, D], BF16, pp)
        ps_ch4 = [psm(f"ps_ch{i}", [128, 128], F32, pp) for i in range(NCH)]
        ps_aph = [psm(f"ps_aph{i}", [128, 512], F32, pp) for i in range(3)]
        ps_bc = psm("ps_bc", [128, NW], F32, pp)
        ps_tp = psm("ps_tp", [128, 1024], BF16, pp)

        pg = SlotGuard()
        _half = [0]
        first_prep_mm = [True]

        def prep_pass(nst, blob_dram, src_fn, ncols, sink_half, sink_bias, src_wait=None):
            for base in range(0, 32, NW):
                cs = [base + i for i in range(NW)]
                wbl = []
                for i in range(NW):
                    sgb = s_gb[i]
                    P.op("sync", lambda s, i=i, nst=nst, c=cs[i]: s.dma_start(
                        out=gblob[i][:, 0:nst * 128].rearrange("p (j f) -> p j f", j=nst),
                        in_=blob_dram.ap()[c].rearrange("j p f -> p j f")),
                        waits=[pg.use(("gblob", i))], inc=sgb, incn=16)
                    wbl.append((sgb, sgb.v))
                cur = [idf_sb] * NW
                wch = [None] * NW
                for j in range(nst):
                    for i in range(NW):
                        extra = []
                        if first_prep_mm[0]:
                            extra = [(sDVE, w_p0_dve), (sACT, w_p0_act)]
                            first_prep_mm[0] = False
                        bk = i % NCH
                        P.op("tensor", lambda t, j=j, i=i, bk=bk, cur=cur[i]: t.matmul(
                            ps_ch4[bk][:, :], gblob[i][:, j * 128:(j + 1) * 128], cur[:],
                            start=True, stop=True),
                            waits=extra + ([wbl[i], pg.use(("ps_ch", bk))]
                                           if j == 0 else [pg.use(("ps_ch", bk))]),
                            inc=sPE)
                        wpe = sPE.v
                        nxt = t_pp[i * 2 + j % 2]
                        if i % 2 == 0:
                            P.op("vector", lambda v, nxt=nxt, bk=bk: v.tensor_copy(nxt[:], ps_ch4[bk][:, :]),
                                 waits=[(sPE, wpe)], inc=sDVE)
                            pg.set(("ps_ch", bk), sDVE, sDVE.v)
                            wch[i] = (sDVE, sDVE.v)
                        else:
                            P.op("scalar", lambda s, nxt=nxt, bk=bk: s.copy(nxt[:], ps_ch4[bk][:, :]),
                                 waits=[(sPE, wpe)], inc=sACT)
                            pg.set(("ps_ch", bk), sACT, sACT.v)
                            wch[i] = (sACT, sACT.v)
                        cur[i] = nxt
                for i in range(NW):
                    pg.set(("gblob", i), sPE, sPE.v)
                wld = []
                for i in range(NW):
                    swr = s_wr[i]
                    P.op("sync", lambda s, i=i, c=cs[i]: s.dma_start(
                        out=src_fn[0](wrow[i]), in_=src_fn[1](c)),
                        waits=[pg.use(("wrow", i))] + list(src_wait or ()),
                        inc=swr, incn=16)
                    wld.append((swr, swr.v))
                ress = []
                for i in range(NW):
                    lhs = t_pp[i * 2 + (nst - 1) % 2]
                    res = {}
                    for h, (o0, o1) in enumerate(((0, 512), (512, 1024))):
                        hs = _half[0] % 3
                        _half[0] += 1
                        P.op("tensor", lambda t, i=i, lhs=lhs, o0=o0, o1=o1, hs=hs: t.matmul(
                            ps_aph[hs][:, :], lhs[:], wrow[i][:, o0:o1], start=True, stop=True),
                            waits=([wld[i], wch[i], pg.use(("aph", hs))] if h == 0
                                   else [pg.use(("aph", hs))]),
                            inc=sPE)
                        res["w%d" % h] = sPE.v
                        res["s%d" % h] = hs
                    pg.set(("wrow", i), sPE, sPE.v)
                    pg.set(("t_pp", i), sPE, sPE.v)
                    ress.append(res)
                    sink_half(cs[i], res)
                if ncols > 1024:
                    for i in range(NW):
                        lhs = t_pp[i * 2 + (nst - 1) % 2]
                        P.op("tensor", lambda t, i=i, lhs=lhs: t.matmul(
                            ps_bc[:, i:i + 1], lhs[:], wrow[i][:, 1024:1025],
                            start=True, stop=True),
                            waits=[pg.use(("bc", i))], inc=sPE)
                    wbias = sPE.v
                    for i in range(NW):
                        pg.set(("wrow", i), sPE, wbias)
                        sink_bias(cs[i], i, wbias)

        def up1_half(c, r):
            k = c % NW
            P.op("vector", lambda v, k=k, r=r: v.tensor_copy(
                wev[k][:, 0:512], ps_aph[r["s0"]][:, :]),
                waits=[(sPE, r["w0"]), pg.use(("wev", k))], inc=sDVE)
            pg.set(("aph", r["s0"]), sDVE, sDVE.v)
            P.op("vector", lambda v, k=k, r=r: v.tensor_copy(
                wev[k][:, 512:1024], ps_aph[r["s1"]][:, :]),
                waits=[(sPE, r["w1"])], inc=sDVE)
            pg.set(("aph", r["s1"]), sDVE, sDVE.v)
            pg.set(("wevh", k), sDVE, sDVE.v)

        def up1_bias(c, i, wbias):
            k = c % NW
            P.op("vector", lambda v, k=k, i=i: v.tensor_copy(
                wev[k][:, D:D + 1], ps_bc[:, i:i + 1]),
                waits=[(sPE, wbias), pg.use(("wevh", k))], inc=sDVE)
            wevd = sDVE.v
            pg.set(("bc", i), sDVE, wevd)
            P.op("scalar", lambda s, c=c, k=k: s.dma_start(
                out=wtmp_up.ap()[c * 128:(c + 1) * 128, :], in_=wev[k][:]),
                waits=[(sDVE, wevd)], inc=s_wtup[k], incn=16)
            pg.set(("wev", k), s_wtup[k], s_wtup[k].v)

        prep_pass(7, up_lo_b,
                  (lambda wr: wr[:], lambda c: wexpT_aug.ap()[c * 128:(c + 1) * 128, :]),
                  D + 1, up1_half, up1_bias)
        w_up1 = [(s, s.v) for s in s_wtup]

        def up2_half(c2, r):
            k = c2 % 2
            P.op("scalar", lambda s, k=k, r=r: s.copy(wbf[k][:, 0:512], ps_aph[r["s0"]][:, :]),
                 waits=[(sPE, r["w0"]), pg.use(("wbf", k))], inc=sACT)
            pg.set(("aph", r["s0"]), sACT, sACT.v)
            P.op("scalar", lambda s, k=k, r=r: s.copy(wbf[k][:, 512:1024], ps_aph[r["s1"]][:, :]),
                 waits=[(sPE, r["w1"])], inc=sACT)
            wevb = sACT.v
            pg.set(("aph", r["s1"]), sACT, wevb)
            for dc in range(8):
                P.op("tensor", lambda t, dc=dc, k=k: t.transpose(
                    ps_tp[:, dc * 128:(dc + 1) * 128], wbf[k][:, dc * 128:(dc + 1) * 128],
                    idbf_sb[:]),
                    waits=([(sACT, wevb), pg.use("tp")] if dc == 0 else []),
                    inc=(sPE if dc == 7 else None))
            wtp = sPE.v
            pg.set(("wbf", k), sPE, wtp)
            P.op("vector", lambda v: v.tensor_copy(wtr[:], ps_tp[:, :]),
                 waits=[(sPE, wtp), pg.use("wtr")], inc=sDVE)
            wtrd = sDVE.v
            pg.set("tp", sDVE, wtrd)
            P.op("scalar", lambda s, c2=c2: s.dma_start(
                out=w1blob.ap()[c2].rearrange("dc p f -> p dc f"),
                in_=wtr[:].rearrange("p (dc f) -> p dc f", dc=8)),
                waits=[(sDVE, wtrd)], inc=s_w1b, incn=16)
            pg.set("wtr", s_w1b, s_w1b.v)

        def up2_bias(c2, i, wbias):
            P.op("vector", lambda v, c2=c2, i=i: v.tensor_add(
                vcol_sb[:, c2:c2 + 1], ps_bc[:, i:i + 1], vcol_sb[:, c2:c2 + 1]),
                waits=[(sPE, wbias), (s_cst, w_consts), (sDVE, sDVE.v)], inc=sDVE)
            pg.set(("bc", i), sDVE, sDVE.v)

        prep_pass(5, up_hi_b,
                  (lambda wr: wr[:],
                   lambda c: wtmp_up.ap().rearrange("(p2 c2x) d -> c2x p2 d", c2x=32)[c]),
                  D + 1, up2_half, up2_bias, src_wait=w_up1)
        w_w1 = (s_w1b, s_w1b.v)

        def dna_half(c2, r):
            k = c2 % NW
            P.op("vector", lambda v, k=k, r=r: v.tensor_copy(
                wev[k][:, 0:512], ps_aph[r["s0"]][:, :]),
                waits=[(sPE, r["w0"]), pg.use(("wev", k))], inc=sDVE)
            pg.set(("aph", r["s0"]), sDVE, sDVE.v)
            P.op("vector", lambda v, k=k, r=r: v.tensor_copy(
                wev[k][:, 512:1024], ps_aph[r["s1"]][:, :]),
                waits=[(sPE, r["w1"])], inc=sDVE)
            wevd = sDVE.v
            pg.set(("aph", r["s1"]), sDVE, wevd)
            P.op("scalar", lambda s, c2=c2, k=k: s.dma_start(
                out=wtmp_dn.ap().rearrange("(p2 c2x) d -> c2x p2 d", c2x=32)[c2],
                in_=wev[k][:, 0:D]),
                waits=[(sDVE, wevd)], inc=s_wtdn[k], incn=16)
            pg.set(("wev", k), s_wtdn[k], s_wtdn[k].v)

        prep_pass(5, dn_hi_b,
                  (lambda wr: wr[:, 0:D], lambda c: wconH.ap()[c * 128:(c + 1) * 128, :]),
                  D, dna_half, None)
        w_dna = [(s, s.v) for s in s_wtdn]

        def dnb_half(c, r):
            k = c % 2
            P.op("scalar", lambda s, k=k, r=r: s.copy(wbf[k][:, 0:512], ps_aph[r["s0"]][:, :]),
                 waits=[(sPE, r["w0"]), pg.use(("wbf", k))], inc=sACT)
            pg.set(("aph", r["s0"]), sACT, sACT.v)
            P.op("scalar", lambda s, k=k, r=r: s.copy(wbf[k][:, 512:1024], ps_aph[r["s1"]][:, :]),
                 waits=[(sPE, r["w1"])], inc=sACT)
            wevb = sACT.v
            pg.set(("aph", r["s1"]), sACT, wevb)
            P.op("scalar", lambda s, c=c, k=k: [
                s.dma_start(
                    out=w2blob.ap().rearrange("(b ya) d -> ya b d", b=32)[c * 4 + a],
                    in_=wbf[k][a * 32:(a + 1) * 32, :])
                for a in range(4)],
                waits=[(sACT, wevb)], inc=s_w2b[c % NW], incn=16, nlist=4)
            pg.set(("wbf", k), s_w2b[c % NW], s_w2b[c % NW].v)

        prep_pass(7, dn_lo_b,
                  (lambda wr: wr[:, 0:D], lambda c: wtmp_dn.ap()[c * 128:(c + 1) * 128, :]),
                  D, dnb_half, None, src_wait=w_dna)
        w_w2 = [(s, s.v) for s in s_w2b]
        P.op("vector", lambda v: v.reduce_max(kmax_sb[:], tiny[:, 0:4], axis=AX.X),
             waits=[(s_kmo, w_kmrb)], inc=sDVE)
        w_rm = sDVE.v
        P.op("vector", lambda v: v.tensor_mul(kmax_sb[:], kmax_sb[:], qmax_sb[:]),
             waits=[(sDVE, w_rm)], inc=sDVE)
        w_m2 = sDVE.v
        P.op("scalar", lambda s: s.activation(kmax_sb[:], kmax_sb[:], AF.Sqrt),
             waits=[(sDVE, w_m2)], inc=sACT)
        w_m1 = sACT.v
        P.op("vector", lambda v: v.tensor_scalar_mul(kmax_sb[:], kmax_sb[:], -SCALE),
             waits=[(sACT, w_m1)], inc=sDVE)
        w_negm = sDVE.v
        P.op("tensor", lambda t: t.matmul(ps_ch4[0][:, 0:1], ones_f1[:], kmax_sb[:], start=True, stop=True),
             waits=[(sDVE, w_negm), pg.use(("ps_ch", 0))], inc=sPE)
        w_eb = sPE.v
        P.op("vector", lambda v: v.tensor_copy(expb_sb[:], ps_ch4[0][:, 0:1]),
             waits=[(sPE, w_eb)], inc=sDVE)
        w_expb = sDVE.v
        pp.close()
        p0.close()
        P.fence(exclude=(sCC, s_kmo))
        if stop_after == "prep":
            _stop_here()
            return nc

        pa = ExitStack()
        woT_sb = sb("woT_sb", [128, 8 * D], BF16, pa)
        kring = sb("kring", [128, NRING * 8 * SLOT], BF16, pa)
        vring = sb("vring", [128, NRING * 4 * D], BF16, pa)
        qgrp = sb("qgrp", [128, 8 * 512], BF16, pa)
        att = sb("att", [128, CPS * 512], BF16, pa)
        oacc = sb("oacc", [128, 4 * D], F32, pa)
        obf = sb("obf", [128, D], BF16, pa)
        ofm = sb("ofm", [128, 8 * 512], BF16, pa)
        ao = sb("ao", [128, 8 * 512], F32, pa)
        aob = sb("aob", [128, 512], BF16, pa)
        aos = sb("aos", [128, 512], BF16, pa)
        xres = sb("xres", [128, 8 * 512], F32, pa)
        linv = sb("linv", [128, 4], F32, pa)
        tsc = sb("tsc", [128, 512], F32, pa)
        bcm_sb = sb("bcm_sb", [128, 512], F32, pa)
        bcr_sb = sb("bcr_sb", [128, 512], F32, pa)

        ps_sc = [psm(f"ps_sc{i}", [128, 512], F32, pa) for i in range(2)]
        ps_pv = [psm(f"ps_pv{i}", [128, D], F32, pa) for i in range(2)]
        ps_l = psm("ps_l", [128, 512], F32, pa)
        ps_tr2 = psm("ps_tr2", [128, 1024], BF16, pa)
        ps_u0 = ps_pv[0]
        ps_u1 = ps_pv[1]

        P.op("gpsimd", lambda g: g.dma_start(
            out=woT_sb[:].rearrange("p (c e) -> p c e", c=8),
            in_=woT.ap().rearrange("(c p) e -> p c e", p=128)),
            inc=s_wc, incn=16)
        w_wocast = s_wc.v
        ag = SlotGuard()
        ring_ready = {}
        ring_free = {}

        def ring_load(sg):
            rb = sg % NRING
            s_abs = (sg % NSLOT) * SLOT
            rank = s_abs // T
            soff = s_abs % T
            wl = [(sCC, w_cc)]
            if sg >= NRING:
                wl.append((sPE, ring_free[sg - NRING]))
            sr = s_ring[rb]
            P.op("sync", lambda s, rb=rb, rank=rank, soff=soff: s.dma_start(
                out=kring[:, rb * 8 * SLOT:(rb + 1) * 8 * SLOT].rearrange("p (dc f) -> p dc f", dc=8),
                in_=kv_cc_out.ap()[rank * CCR:rank * CCR + D, soff:soff + SLOT]
                    .rearrange("(dc p) f -> p dc f", p=128)),
                waits=wl, inc=sr, incn=16)
            P.op("sync", lambda s, rb=rb, rank=rank, soff=soff: s.dma_start(
                out=vring[:, rb * 4 * D:(rb + 1) * 4 * D].rearrange("p (n e) -> p n e", n=4),
                in_=vflat(kv_cc_out.ap(), rank * CCR, soff, SLOT)
                    .rearrange("(n p) e -> p n e", p=128)),
                inc=sr, incn=16)
            ring_ready[sg] = (sr, sr.v)

        TOTSLOT = NG * NSLOT
        sg_next = 0
        while sg_next < min(NRING, TOTSLOT):
            ring_load(sg_next)
            sg_next += 1

        x1_store_w = {}
        for g in range(NG):
            P.op("sync", lambda s, g=g: s.dma_start(
                out=qgrp[:].rearrange("p (c f) -> p c f", c=8),
                in_=qspill.ap().rearrange("(c p) f -> p c f", p=128)[:, :, g * 512:(g + 1) * 512]),
                waits=[(s_qsp, s_qsp.v), ag.use("qgrp")], inc=s_qg, incn=16)
            wqg = (s_qg, s_qg.v)

            for sp in range(NSPAN):
                gsp = g * NSPAN + sp
                for cc_ in range(CPS):
                    sg = g * NSLOT + sp * SPS + cc_ // 4
                    rb = sg % NRING
                    sc = cc_ % 4
                    slot = cc_ % 2
                    wl = [ring_ready[sg]]
                    if cc_ == 0:
                        wl += [wqg, (sDVE, w_expb)]
                    wl.append(ag.use(("sc", slot)))
                    for dc in range(8):
                        P.op("tensor", lambda t, rb=rb, sc=sc, dc=dc, slot=slot: t.matmul(
                            ps_sc[slot][:, :],
                            kring[:, rb * 8 * SLOT + dc * SLOT + sc * 128:
                                   rb * 8 * SLOT + dc * SLOT + sc * 128 + 128],
                            qgrp[:, dc * 512:(dc + 1) * 512],
                            start=(dc == 0), stop=(dc == 7)),
                            waits=(wl if dc == 0 else []),
                            inc=(sPE if dc == 7 else None))
                    wsc_ = sPE.v
                    P.op("scalar", lambda s, cc_=cc_, slot=slot: s.activation(
                        att[:, cc_ * 512:(cc_ + 1) * 512], ps_sc[slot][:, :], AF.Exp,
                        bias=expb_sb[:], scale=SCALE),
                        waits=[(sPE, wsc_)], inc=sACT)
                    ag.set(("sc", slot), sACT, sACT.v)
                w_exps = sACT.v
                for cc_ in range(CPS):
                    P.op("tensor", lambda t, cc_=cc_, sp=sp: t.matmul(
                        ps_l[0:1, :], ones_bf[:], att[:, cc_ * 512:(cc_ + 1) * 512],
                        start=(sp == 0 and cc_ == 0), stop=(sp == NSPAN - 1 and cc_ == CPS - 1)),
                        waits=([(sACT, w_exps), ag.use("psl")] if cc_ == 0 else []),
                        inc=(sPE if cc_ == CPS - 1 else None))
                for qt in range(4):
                    slot = qt % 2
                    wl = [(sACT, w_exps), ag.use(("pv", slot))]
                    for cc_ in range(CPS):
                        sg = g * NSLOT + sp * SPS + cc_ // 4
                        rb = sg % NRING
                        for eh in range(2):
                            P.op("tensor", lambda t, qt=qt, cc_=cc_, eh=eh, slot=slot, rb=rb: t.matmul(
                                ps_pv[slot][:, eh * 512:(eh + 1) * 512],
                                att[:, cc_ * 512 + qt * 128: cc_ * 512 + qt * 128 + 128],
                                vring[:, rb * 4 * D + (cc_ % 4) * D + eh * 512:
                                       rb * 4 * D + (cc_ % 4) * D + (eh + 1) * 512],
                                start=(cc_ == 0), stop=(cc_ == CPS - 1)),
                                waits=(wl if (cc_ == 0 and eh == 0) else []),
                                inc=(sPE if (cc_ == CPS - 1 and eh == 1) else None))
                    wpv = sPE.v
                    if sp == 0:
                        P.op("vector", lambda v, qt=qt, slot=slot: [
                            v.tensor_copy(oacc[:, qt * D:qt * D + 512], ps_pv[slot][:, 0:512]),
                            v.tensor_copy(oacc[:, qt * D + 512:(qt + 1) * D], ps_pv[slot][:, 512:D]),
                        ], waits=[(sPE, wpv), ag.use(("oacc", qt))], inc=sDVE, incn=1, nlist=2)
                    else:
                        P.op("vector", lambda v, qt=qt, slot=slot: [
                            v.tensor_add(oacc[:, qt * D:qt * D + 512],
                                         oacc[:, qt * D:qt * D + 512], ps_pv[slot][:, 0:512]),
                            v.tensor_add(oacc[:, qt * D + 512:(qt + 1) * D],
                                         oacc[:, qt * D + 512:(qt + 1) * D], ps_pv[slot][:, 512:D]),
                        ], waits=[(sPE, wpv)], inc=sDVE, incn=1, nlist=2)
                    ag.set(("pv", slot), sDVE, sDVE.v)
                w_pvspan = sPE.v
                for j in range(SPS):
                    ring_free[g * NSLOT + sp * SPS + j] = w_pvspan
                while sg_next < min(TOTSLOT, g * NSLOT + (sp + 1) * SPS + NRING):
                    if sg_next - NRING >= 0 and sg_next - NRING not in ring_free:
                        break
                    ring_load(sg_next)
                    sg_next += 1
            w_lastflush = sDVE.v
            ag.set("qgrp", sPE, sPE.v)

            P.op("vector", lambda v: v.tensor_copy(row_t[:, :], ps_l[0:1, :]),
                 waits=[(sPE, sPE.v), rowg.use("rows_act")], inc=sDVE)
            w_lrow = sDVE.v
            for qt in range(4):
                P.op("tensor", lambda t, qt=qt: t.matmul(
                    ps_l[0:128, qt:qt + 1], row_t[0:1, qt * 128:(qt + 1) * 128], one_f[:],
                    start=(qt == 0), stop=(qt == 3)),
                    waits=([(sDVE, w_lrow)] if qt == 0 else []),
                    inc=(sPE if qt == 3 else None))
            w_lcols = sPE.v
            P.op("vector", lambda v: v.reciprocal(linv[:, :], ps_l[0:128, 0:4]),
                 waits=[(sPE, w_lcols)], inc=sDVE)
            w_linv = sDVE.v
            ag.set("psl", sDVE, w_linv)

            for qt in range(4):
                P.op("vector", lambda v, qt=qt: v.tensor_scalar(
                    obf[:], oacc[:, qt * D:(qt + 1) * D], linv[:, qt:qt + 1], None, ALU.mult),
                    waits=[(sDVE, w_linv), (sDVE, w_lastflush), ag.use("obf")],
                    inc=sDVE)
                w_obf = sDVE.v
                ag.set(("oacc", qt), sDVE, w_obf)
                for ec in range(8):
                    P.op("tensor", lambda t, ec=ec: t.transpose(
                        ps_tr2[:, ec * 128:(ec + 1) * 128], obf[:, ec * 128:(ec + 1) * 128],
                        idbf_sb[:]),
                        waits=([(sDVE, w_obf), ag.use("tr")] if ec == 0 else []),
                        inc=(sPE if ec == 7 else None))
                wtp = sPE.v
                P.op("vector", lambda v, qt=qt: v.tensor_copy(
                    ofm[:].rearrange("p (ec q) -> p ec q", ec=8)[:, :, qt * 128:(qt + 1) * 128],
                    ps_tr2[:, :].rearrange("p (ec q) -> p ec q", ec=8)),
                    waits=[(sPE, wtp), ag.use("ofm")], inc=sDVE)
                ag.set("tr", sDVE, sDVE.v)
                ag.set("obf", sPE, sPE.v)
            w_ofm = sDVE.v

            P.op("sync", lambda s, g=g: s.dma_start(
                out=xres[:].rearrange("p (c f) -> p c f", c=8),
                in_=x_fm.ap().rearrange("(c p) f -> p c f", p=128)[:, :, g * 512:(g + 1) * 512]),
                waits=[ag.use("xres")], inc=s_xres, incn=16)
            w_xres = (s_xres, s_xres.v)

            for es in range(8):
                slot = es % 2
                for ec in range(8):
                    P.op("tensor", lambda t, es=es, ec=ec, slot=slot: t.matmul(
                        ps_sc[slot][:, :], woT_sb[:, ec * D + es * 128:ec * D + es * 128 + 128],
                        ofm[:, ec * 512:(ec + 1) * 512],
                        start=(ec == 0), stop=(ec == 7)),
                        waits=([(sDVE, w_ofm), (s_wc, w_wocast), ag.use(("sc", slot))] if ec == 0 else []),
                        inc=(sPE if ec == 7 else None))
                w_aomm = sPE.v
                P.op("scalar", lambda s, es=es, slot=slot: s.activation(
                    ao[:, es * 512:(es + 1) * 512], ps_sc[slot][:, :], AF.Identity,
                    bias=bo_sb[:, es:es + 1]),
                    waits=[(sPE, w_aomm), ag.use("ao")], inc=sACT)
                P.op("scalar", lambda s, es=es, slot=slot: s.activation(
                    aob[:], ps_sc[slot][:, :], AF.Identity, bias=bo_sb[:, es:es + 1]),
                    waits=[ag.use("aob")], inc=sACT)
                w_aob = sACT.v
                P.op("scalar", lambda s, es=es, slot=slot: s.activation(
                    aos[:], ps_sc[slot][:, :], AF.Square, bias=bo_sb[:, es:es + 1]),
                    waits=[ag.use("aos")], inc=sACT)
                ag.set(("sc", slot), sACT, sACT.v)
                w_aos = sACT.v
                P.op("tensor", lambda t, es=es: t.matmul(
                    ps_u0[0:1, 0:512], ones_bf[:], aob[:], start=(es == 0), stop=(es == 7)),
                    waits=[(sACT, w_aob)] + ([(sDVE, w_lastflush), ag.use(("pv", 0))] if es == 0 else []),
                    inc=sPE)
                ag.set("aob", sPE, sPE.v)
                P.op("tensor", lambda t, es=es: t.matmul(
                    ps_u1[0:1, 0:512], ones_bf[:], aos[:], start=(es == 0), stop=(es == 7)),
                    waits=[(sACT, w_aos)] + ([ag.use(("pv", 1))] if es == 0 else []), inc=sPE)
                ag.set("aos", sPE, sPE.v)
            w_stato = sPE.v
            bc1o = ln_rows(w_stato, 512, ps_u0, ps_u1, ps_sc[0], ps_sc[1])
            ag.set(("pv", 0), sDVE, sDVE.v - 5)
            ag.set(("pv", 1), sDVE, sDVE.v - 5)
            P.op("vector", lambda v: v.tensor_copy(bcm_sb[:], ps_sc[0][:, :]),
                 waits=[(sPE, bc1o), ag.use("bcm")], inc=sDVE)
            P.op("vector", lambda v: v.tensor_copy(bcr_sb[:], ps_sc[1][:, :]),
                 inc=sDVE)
            w_bcc = sDVE.v
            ag.set(("sc", 0), sDVE, w_bcc)
            ag.set(("sc", 1), sDVE, w_bcc)
            for c in range(8):
                P.op("gpsimd", lambda gp, c=c: gp.tensor_mul(
                    tsc[:], ao[:, c * 512:(c + 1) * 512], bcr_sb[:]),
                    waits=([(sDVE, w_bcc), w_xres, ag.use("gx1")] if c == 0 else [(sGP, sGP.v)]),
                    inc=sGP)
                w1_ = sGP.v
                P.op("gpsimd", lambda gp, c=c: gp.tensor_sub(
                    ao[:, c * 512:(c + 1) * 512], tsc[:], bcm_sb[:]),
                    waits=[(sGP, w1_)], inc=sGP)
                w2_ = sGP.v
                P.op("gpsimd", lambda gp, c=c: gp.tensor_add(
                    ao[:, c * 512:(c + 1) * 512], ao[:, c * 512:(c + 1) * 512],
                    xres[:, c * 512:(c + 1) * 512]),
                    waits=[(sGP, w2_)], inc=sGP)
            w_x1 = sGP.v
            ag.set("xres", sGP, w_x1)
            ag.set("bcm", sGP, w_x1)
            ag.set("ao", sGP, w_x1)
            ag.set("gx1", sGP, w_x1)
            P.op("scalar", lambda s, g=g: s.dma_start(
                out=x1sp.ap().rearrange("(c p) f -> p c f", p=128)[:, :, g * 512:(g + 1) * 512],
                in_=ao[:].rearrange("p (c f) -> p c f", c=8)),
                waits=[(sGP, w_x1)], inc=s_x1s, incn=16)
            x1_store_w[g] = (s_x1s, s_x1s.v)
            ag.set("ao", s_x1s, s_x1s.v)
            ag.set("ofm", sPE, w_stato)

        pa.close()
        P.fence()
        if stop_after == "attn":
            _stop_here()
            return nc

        pf = ExitStack()
        x1t2 = [sb(f"x1t{i}", [128, 8 * 512], F32, pf) for i in range(2)]
        x1bf = sb("x1bf", [128, 512], BF16, pf)
        x1sq = sb("x1sq", [128, 512], BF16, pf)
        xn2 = sb("xn2", [128, 8 * 512], BF16, pf)
        w1s = [sb(f"w1s{i}", [128, 8 * 128], BF16, pf) for i in range(4)]
        w2s = [sb(f"w2s{i}", [128, D], BF16, pf) for i in range(4)]
        gt = sb("gt", [128, 32 * 512], BF16, pf)
        yv = sb("yv", [128, 8 * 512], F32, pf)
        yb = sb("yb", [128, 512], BF16, pf)
        ysq = sb("ysq", [128, 512], BF16, pf)
        tf = sb("tf", [128, 512], F32, pf)

        fbm_sb = sb("fbm_sb", [128, 512], F32, pf)
        fbr_sb = sb("fbr_sb", [128, 512], F32, pf)
        ps_f = [psm(f"ps_f{i}", [128, 512], F32, pf) for i in range(2)]
        ps_fb = [psm(f"ps_fb{i}", [128, 512], F32, pf) for i in range(2)]
        ps_dn = [psm(f"ps_dn{i}", [128, 512], F32, pf) for i in range(4)]

        fg = SlotGuard()
        for ft in range(NFT):
            x1t = x1t2[ft % 2]
            P.op("sync", lambda s, ft=ft, x1t=x1t: s.dma_start(
                out=x1t[:].rearrange("p (c f) -> p c f", c=8),
                in_=x1sp.ap().rearrange("(c p) f -> p c f", p=128)[:, :, ft * 512:(ft + 1) * 512]),
                waits=[x1_store_w[ft], fg.use(("x1t", ft % 2))], inc=s_x1t, incn=16)
            w_x1t = (s_x1t, s_x1t.v)
            for c in range(8):
                P.op("vector", lambda v, c=c, x1t=x1t: v.tensor_copy(x1bf[:], x1t[:, c * 512:(c + 1) * 512]),
                     waits=[w_x1t, fg.use("x1bf")], inc=sDVE)
                w_cb = sDVE.v
                P.op("scalar", lambda s, c=c, x1t=x1t: s.activation(x1sq[:], x1t[:, c * 512:(c + 1) * 512], AF.Square),
                     waits=[w_x1t, fg.use("x1sq")], inc=sACT)
                w_sb2 = sACT.v
                P.op("tensor", lambda t, c=c: t.matmul(
                    ps_f[0][0:1, :], ones_bf[:], x1bf[:], start=(c == 0), stop=(c == 7)),
                    waits=[(sDVE, w_cb)] + ([fg.use("psf")] if c == 0 else []),
                    inc=sPE)
                fg.set("x1bf", sPE, sPE.v)
                P.op("tensor", lambda t, c=c: t.matmul(
                    ps_f[1][0:1, :], ones_bf[:], x1sq[:], start=(c == 0), stop=(c == 7)),
                    waits=[(sACT, w_sb2)], inc=sPE)
                fg.set("x1sq", sPE, sPE.v)
            bcf = ln_rows(sPE.v, 512, ps_f[0], ps_f[1], ps_fb[0], ps_fb[1])
            fg.set("psf", sDVE, sDVE.v - 5)
            for c in range(8):
                P.op("vector", lambda v, c=c, x1t=x1t: v.tensor_mul(
                    tf[:], x1t[:, c * 512:(c + 1) * 512], ps_fb[1][:, :]),
                    waits=([(sPE, bcf), fg.use("tf")] if c == 0 else [(sDVE, sDVE.v)]),
                    inc=sDVE)
                wmu = sDVE.v
                P.op("vector", lambda v, c=c: v.tensor_sub(
                    xn2[:, c * 512:(c + 1) * 512], tf[:], ps_fb[0][:, :]),
                    waits=[(sDVE, wmu)] + ([fg.use("xn2")] if c == 0 else []), inc=sDVE)
            w_xn2 = sDVE.v
            fg.set("tf", sDVE, w_xn2)

            for hc in range(32):
                wb = w1s[hc % 4]
                P.op("sync", lambda s, hc=hc, wb=wb: s.dma_start(
                    out=wb[:].rearrange("p (dc f) -> p dc f", dc=8),
                    in_=w1blob.ap()[hc].rearrange("dc p f -> p dc f")),
                    waits=[w_w1, fg.use(("w1", hc % 4))], inc=s_w1s[hc % 4], incn=16)
                wld = (s_w1s[hc % 4], s_w1s[hc % 4].v)
                slot = hc % 2
                for dc in range(8):
                    P.op("tensor", lambda t, hc=hc, dc=dc, wb=wb, slot=slot: t.matmul(
                        ps_f[slot][:, :], wb[:, dc * 128:(dc + 1) * 128],
                        xn2[:, dc * 512:(dc + 1) * 512],
                        start=(dc == 0), stop=(dc == 7)),
                        waits=([(sDVE, w_xn2), wld, fg.use(("psf2", slot))] if dc == 0 else []),
                        inc=(sPE if dc == 7 else None))
                fg.set(("w1", hc % 4), sPE, sPE.v)
                wmm = sPE.v
                P.op("scalar", lambda s, hc=hc, slot=slot: s.activation(
                    gt[:, hc * 512:(hc + 1) * 512], ps_f[slot][:, :], AF.Gelu,
                    bias=vcol_sb[:, hc:hc + 1]),
                    waits=[(sPE, wmm)] + ([fg.use("gt")] if hc == 0 else []), inc=sACT)
                fg.set(("psf2", slot), sACT, sACT.v)
            w_gt = sACT.v
            fg.set("psf", sACT, w_gt)

            for half in range(2):
                for hc in range(32):
                    pi = (hc // 2) % 4
                    wb = w2s[pi]
                    if hc % 2 == 0:
                        P.op("sync", lambda s, hc=hc, wb=wb, half=half: s.dma_start(
                            out=wb[:].rearrange("p (two f) -> p two f", two=2),
                            in_=w2blob.ap()[hc * 128:(hc + 2) * 128, half * 512:(half + 1) * 512]
                                .rearrange("(two p) f -> p two f", p=128)),
                            waits=list(w_w2) + [fg.use(("w2", pi))], inc=s_w2s[pi], incn=16)
                    wld = (s_w2s[pi], s_w2s[pi].v)
                    off = (hc % 2) * 512
                    for ds in range(4):
                        P.op("tensor", lambda t, hc=hc, ds=ds, wb=wb, off=off: t.matmul(
                            ps_dn[ds][:, :], wb[:, off + ds * 128: off + (ds + 1) * 128],
                            gt[:, hc * 512:(hc + 1) * 512],
                            start=(hc == 0), stop=(hc == 31)),
                            waits=(([(sACT, w_gt)] if hc == 0 and ds == 0 and half == 0 else []) +
                                   ([wld] if ds == 0 else []) +
                                   ([fg.use(("psdn", ds))] if hc == 0 else [])),
                            inc=(sPE if ds == 3 else None))
                    fg.set(("w2", pi), sPE, sPE.v)
                w_dn = sPE.v
                for ds in range(4):
                    es = half * 4 + ds
                    P.op("scalar", lambda s, ds=ds, es=es: s.activation(
                        yv[:, es * 512:(es + 1) * 512], ps_dn[ds][:, :], AF.Identity,
                        bias=ob_sb[:, es:es + 1]),
                        waits=[(sPE, w_dn), (sDVE, w_ob)] + ([fg.use("yv")] if es == 0 else []),
                        inc=sACT)
                    P.op("scalar", lambda s, ds=ds, es=es: s.activation(
                        yb[:], ps_dn[ds][:, :], AF.Identity, bias=ob_sb[:, es:es + 1]),
                        waits=[fg.use("yb")], inc=sACT)
                    w_yb = sACT.v
                    P.op("scalar", lambda s, ds=ds, es=es: s.activation(
                        ysq[:], ps_dn[ds][:, :], AF.Square, bias=ob_sb[:, es:es + 1]),
                        waits=[fg.use("ysq")], inc=sACT)
                    w_ysq = sACT.v
                    fg.set(("psdn", ds), sACT, sACT.v)
                    P.op("tensor", lambda t, es=es: t.matmul(
                        ps_f[0][0:1, :], ones_bf[:], yb[:], start=(es == 0), stop=(es == 7)),
                        waits=[(sACT, w_yb)] + ([fg.use("psf")] if es == 0 else []),
                        inc=sPE)
                    fg.set("yb", sPE, sPE.v)
                    P.op("tensor", lambda t, es=es: t.matmul(
                        ps_f[1][0:1, :], ones_bf[:], ysq[:], start=(es == 0), stop=(es == 7)),
                        waits=[(sACT, w_ysq)], inc=sPE)
                    fg.set("ysq", sPE, sPE.v)
            fg.set("gt", sPE, sPE.v)
            bcf2 = ln_rows(sPE.v, 512, ps_f[0], ps_f[1], ps_fb[0], ps_fb[1])
            fg.set("psf", sDVE, sDVE.v - 5)
            P.op("vector", lambda v: v.tensor_copy(fbm_sb[:], ps_fb[0][:, :]),
                 waits=[(sPE, bcf2), fg.use("fbm")], inc=sDVE)
            P.op("vector", lambda v: v.tensor_copy(fbr_sb[:], ps_fb[1][:, :]),
                 inc=sDVE)
            w_fbc = sDVE.v
            for c in range(8):
                P.op("gpsimd", lambda gp, c=c: gp.tensor_mul(
                    tf[:], yv[:, c * 512:(c + 1) * 512], fbr_sb[:]),
                    waits=([(sDVE, w_fbc), fg.use("tf")] if c == 0 else [(sGP, sGP.v)]),
                    inc=sGP)
                w1_ = sGP.v
                P.op("gpsimd", lambda gp, c=c: gp.tensor_sub(
                    yv[:, c * 512:(c + 1) * 512], tf[:], fbm_sb[:]),
                    waits=[(sGP, w1_)], inc=sGP)
                w2_ = sGP.v
                P.op("gpsimd", lambda gp, c=c, x1t=x1t: gp.tensor_add(
                    yv[:, c * 512:(c + 1) * 512], yv[:, c * 512:(c + 1) * 512],
                    x1t[:, c * 512:(c + 1) * 512]),
                    waits=[(sGP, w2_)], inc=sGP)
            w_y = sGP.v
            fg.set(("x1t", ft % 2), sGP, w_y)
            fg.set("tf", sGP, w_y)
            fg.set("fbm", sGP, w_y)
            P.op("scalar", lambda s, ft=ft: s.dma_start(
                out=y_fm.ap().rearrange("(c p) f -> p c f", p=128)[:, :, ft * 512:(ft + 1) * 512],
                in_=yv[:].rearrange("p (c f) -> p c f", c=8)),
                waits=[(sGP, w_y)], inc=s_yout, incn=16)
            fg.set("yv", s_yout, s_yout.v)
        pf.close()

        P.wait("scalar", (s_yout, s_yout.v))

        block = ctx.enter_context(nc.Block())
        P.replay(block)

    return nc


def _prep_inputs(L0_attn_x, L0_attn_w_q, L0_attn_b_q, L0_attn_w_k, L0_attn_b_k,
                 L0_attn_w_v, L0_attn_b_v, L0_attn_w_out, L0_attn_b_out,
                 L0_ffn_W_exp, L0_ffn_b_exp, L0_ffn_up_weights, L0_ffn_up_bias,
                 L0_ffn_down_weights, L0_ffn_W_con, L0_ffn_b_con, L0_ffn_down_bias,
                 T=4096):
    x = np.asarray(L0_attn_x, np.float32).reshape(-1, D)
    up_lo, up_hi, dn_hi, dn_lo = _host_blobs(
        np.asarray(L0_ffn_up_weights, np.float32),
        np.asarray(L0_ffn_down_weights, np.float32))
    wexpT_aug = np.ascontiguousarray(
        np.concatenate([np.asarray(L0_ffn_W_exp, np.float32).T,
                        np.asarray(L0_ffn_b_exp, np.float32)[:, None]], axis=1))
    idx = np.arange(H)
    hp_rows = (idx % 128) * 32 + idx // 128
    wconH = np.ascontiguousarray(np.asarray(L0_ffn_W_con, np.float32)[hp_rows])
    shared = dict(
        wqT=np.ascontiguousarray(np.asarray(L0_attn_w_q, np.float32).T),
        wkT=np.ascontiguousarray(np.asarray(L0_attn_w_k, np.float32).T),
        wvT=np.ascontiguousarray(np.asarray(L0_attn_w_v, np.float32).T),
        woT=np.ascontiguousarray(np.asarray(L0_attn_w_out, np.float32).T),
        bq8=_col8(L0_attn_b_q), bk8=_col8(L0_attn_b_k), bo8=_col8(L0_attn_b_out),
        bv_row=np.ascontiguousarray(np.asarray(L0_attn_b_v, np.float32)[None, :]),
        id_bf=np.eye(128, dtype=BF),
        idf=np.eye(128, dtype=np.float32),
        wexpT_aug=wexpT_aug, wconH=wconH,
        up_lo_b=up_lo, up_hi_b=up_hi, dn_hi_b=dn_hi, dn_lo_b=dn_lo,
        upb_hp=np.ascontiguousarray(np.asarray(L0_ffn_up_bias, np.float32).reshape(128, 32)),
        bc8=_col8(L0_ffn_b_con), bd8=_col8(L0_ffn_down_bias),
    )
    in_maps = []
    n = x.shape[0] // T
    for c in range(n):
        m = dict(shared)
        m["x_fm"] = np.ascontiguousarray(x[c * T:(c + 1) * T].T)
        in_maps.append(m)
    return in_maps


def kernel(**inputs):
    T = 4096
    in_maps = _prep_inputs(**inputs, T=T)
    nc = build_nc(T)
    res = run_bass_kernel_spmd(nc, in_maps, core_ids=list(range(NCORES)))
    cols = [res.results[c]["y_fm"] for c in range(NCORES)]
    full = np.concatenate(cols, axis=1)
    return np.ascontiguousarray(full.T).reshape(2, 16384, D).astype(np.float32)


# version 10
# speedup vs baseline: 1.0174x; 1.0174x over previous
import numpy as np
import ml_dtypes
from contextlib import ExitStack

import concourse.bass as bass
import concourse.mybir as mybir
from concourse.bass_utils import run_bass_kernel_spmd

F32 = mybir.dt.float32
BF16 = mybir.dt.bfloat16
AF = mybir.ActivationFunctionType
ALU = mybir.AluOpType
AX = mybir.AxisListType

D = 1024
H = 4096
NCORES = 8
EPS = 1e-5
SCALE = 1.0 / 32.0
BF = ml_dtypes.bfloat16


def _lo_block(w_s, stage, c):
    S = 1 << stage
    G = np.zeros((128, 128), dtype=np.float32)
    q = np.arange(128)
    q0 = q[(q & S) == 0]
    i0 = c * 128 + q0
    k = (i0 // (2 * S)) * S + (i0 % S)
    G[q0, q0] = w_s[k, 0, 0]
    G[q0, q0 + S] = w_s[k, 0, 1]
    G[q0 + S, q0] = w_s[k, 1, 0]
    G[q0 + S, q0 + S] = w_s[k, 1, 1]
    return G


def _hi_block(w_s, stage, c2):
    S = 1 << stage
    Sp = S // 32
    Hm = np.zeros((128, 128), dtype=np.float32)
    p2 = np.arange(128)
    i = p2 * 32 + c2
    p0 = p2[(i & S) == 0]
    i0 = p0 * 32 + c2
    k = (i0 // (2 * S)) * S + (i0 % S)
    Hm[p0, p0] = w_s[k, 0, 0]
    Hm[p0, p0 + Sp] = w_s[k, 0, 1]
    Hm[p0 + Sp, p0] = w_s[k, 1, 0]
    Hm[p0 + Sp, p0 + Sp] = w_s[k, 1, 1]
    return Hm


def _host_blobs(up_w, down_w):
    up_lo = np.stack([[_lo_block(up_w[s], s, c) for s in range(6, -1, -1)]
                      for c in range(32)])
    up_hi = np.stack([[_hi_block(up_w[s], s, c2) for s in range(11, 6, -1)]
                      for c2 in range(32)])
    dn_hi = np.stack([[_hi_block(down_w[s], s, c2).T for s in range(7, 12)]
                      for c2 in range(32)])
    dn_lo = np.stack([[_lo_block(down_w[s], s, c).T for s in range(0, 7)]
                      for c in range(32)])
    return (np.ascontiguousarray(up_lo, np.float32),
            np.ascontiguousarray(up_hi, np.float32),
            np.ascontiguousarray(dn_hi, np.float32),
            np.ascontiguousarray(dn_lo, np.float32))


def _col8(b):
    return np.ascontiguousarray(np.asarray(b, np.float32).reshape(8, 128).T)


ENGINES = ("tensor", "vector", "scalar", "gpsimd", "sync")


class Sem:
    def __init__(self, ctx, nc, name):
        self.h = ctx.enter_context(nc.semaphore(name))
        self.v = 0
        self.name = name


class Prog:
    def __init__(self):
        self.ops = {e: [] for e in ENGINES}
        self.sems = []
        self.fence_waits = []
        self.fenced = set()

    def fence(self, exclude=()):
        self.fence_waits = [(s, s.v) for s in self.sems
                            if s.v > 0 and s not in exclude]
        self.fenced = set()

    def op(self, engine, fn, waits=(), inc=None, incn=1, nlist=1):
        w = [x for x in waits if x is not None]
        key = (engine, inc.name if inc is not None else "_")
        if key not in self.fenced:
            self.fenced.add(key)
            w = list(self.fence_waits) + w
        entry = [tuple(w), fn, None]
        if inc is not None:
            inc.v += incn * nlist
            entry[2] = (inc, incn)
        self.ops[engine].append(entry)
        return inc.v if inc is not None else None

    def wait(self, engine, *waits):
        self.ops[engine].append([tuple(w for w in waits if w is not None), None, None])

    def replay(self, block):
        def mk(name):
            ops = self.ops[name]

            def body(eng):
                for waits, fn, inc in ops:
                    for (sem, val) in waits:
                        if val > 0:
                            eng.wait_ge(sem.h, val)
                    if fn is not None:
                        ins = fn(eng)
                        if inc is not None:
                            if not isinstance(ins, list):
                                ins = [ins]
                            for i in ins:
                                i.then_inc(inc[0].h, inc[1])
            return body

        for e in ENGINES:
            getattr(block, e)(mk(e))


class SlotGuard:

    def __init__(self):
        self.d = {}

    def use(self, key):
        return self.d.get(key)

    def set(self, key, sem, val):
        self.d[key] = (sem, val)


def build_nc(T=4096, stop_after=None):
    SEQ = 4 * T
    NT0 = T // 512
    NG = T // 512
    SLOT = 512
    SPAN = 2048
    NSLOT = SEQ // SLOT
    NSPAN = SEQ // SPAN
    SPS = SPAN // SLOT
    CPS = SPAN // 128
    NRING = min(5, NSLOT)
    NFT = T // 512
    TPC = T // 128

    nc = bass.Bass("TRN2", target_bir_lowering=False)

    def din(name, shape, dt=F32):
        return nc.dram_tensor(name, shape, dt, kind="ExternalInput")

    x_fm = din("x_fm", [D, T])
    wqT = din("wqT", [D, D])
    wkT = din("wkT", [D, D])
    wvT = din("wvT", [D, D])
    woT = din("woT", [D, D])
    bq8 = din("bq8", [128, 8])
    bk8 = din("bk8", [128, 8])
    bo8 = din("bo8", [128, 8])
    bv_row = din("bv_row", [1, D])
    id_bf = din("id_bf", [128, 128], BF16)
    idf = din("idf", [128, 128])
    wexpT_aug = din("wexpT_aug", [H, D + 1])
    wconH = din("wconH", [H, D])
    up_lo_b = din("up_lo_b", [32, 7, 128, 128])
    up_hi_b = din("up_hi_b", [32, 5, 128, 128])
    dn_hi_b = din("dn_hi_b", [32, 5, 128, 128])
    dn_lo_b = din("dn_lo_b", [32, 7, 128, 128])
    upb_hp = din("upb_hp", [128, 32])
    bc8 = din("bc8", [128, 8])
    bd8 = din("bd8", [128, 8])

    y_fm = nc.dram_tensor("y_fm", [D, T], F32, kind="ExternalOutput")

    qspill = nc.dram_tensor("qspill", [D, T], BF16, kind="Internal")
    CCR = 2 * D + 1
    kv_cc_in = nc.dram_tensor("kv_cc_in", [CCR, T], BF16, kind="Internal")
    kv_cc_out = nc.dram_tensor("kv_cc_out", [4 * CCR, T], BF16, kind="Internal")
    def vflat(ap, rank_base, tb, ntok):
        r0 = rank_base + D + (tb * D) // T
        nr = (ntok * D) // T
        a = ap[r0:r0 + nr, :]
        if T > D:
            return a.rearrange("a (b c) -> (a b) c", b=T // D)
        if T < D:
            return a.rearrange("(a b) c -> a (b c)", b=D // T)
        return a

    x1sp = nc.dram_tensor("x1sp", [D, T], F32, kind="Internal")
    wtmp_up = nc.dram_tensor("wtmp_up", [H, D + 1], F32, kind="Internal")
    wtmp_dn = nc.dram_tensor("wtmp_dn", [H, D], F32, kind="Internal")
    w1blob = nc.dram_tensor("w1blob", [32, 8, 128, 128], BF16, kind="Internal")
    w2blob = nc.dram_tensor("w2blob", [H, D], BF16, kind="Internal")

    ctx = ExitStack()
    with ctx:
        def sb(name, shape, dt=F32, st=None):
            return (st or ctx).enter_context(nc.sbuf_tensor(name, shape, dt))

        def psm(name, shape, dt=F32, st=None):
            return (st or ctx).enter_context(nc.psum_tensor(name, shape, dt))

        ones_bf = sb("ones_bf", [128, 1], BF16)
        ones_f1 = sb("ones_f1", [1, 128], F32)
        one_f = sb("one_f", [1, 1], F32)
        eps_t = sb("eps_t", [1, 1], F32)
        idbf_sb = sb("idbf_sb", [128, 128], BF16)
        idf_sb = sb("idf_sb", [128, 128], F32)
        bo_sb = sb("bo_sb", [128, 8], F32)
        ob_sb = sb("ob_sb", [128, 8], F32)
        bd_sb = sb("bd_sb", [128, 8], F32)
        vcol_sb = sb("vcol_sb", [128, 32], F32)
        expb_sb = sb("expb_sb", [128, 1], F32)
        kmax_sb = sb("kmax_sb", [1, 1], F32)
        qmax_sb = sb("qmax_sb", [1, 1], F32)
        row_sum = sb("row_sum", [1, 512], F32)
        row_sq = sb("row_sq", [1, 512], F32)
        row_m = sb("row_m", [1, 512], F32)
        row_r = sb("row_r", [1, 512], F32)
        row_mr = sb("row_mr", [1, 512], F32)
        row_t = sb("row_t", [1, 512], F32)
        bvh = [sb(f"bvh{i}", [1, 512], F32) for i in range(2)]
        tiny = sb("tiny", [1, 16], F32)

        P = Prog()

        def Sem_(c, n_, name):
            s = Sem(c, n_, name)
            P.sems.append(s)
            return s
        sCC, sPE, sDVE, sACT, sGP = (Sem_(ctx, nc, n) for n in ["sCC", "sPE", "sDVE", "sACT", "sGP"])
        s_cst, s_wc, s_qsp, s_ksp, s_vsp, s_kmi, s_kmo = (
            Sem_(ctx, nc, n) for n in
            ["s_cst", "s_wc", "s_qsp", "s_ksp", "s_vsp", "s_kmi", "s_kmo"])
        s_xt = [Sem_(ctx, nc, f"s_xt{i}") for i in range(2)]
        s_gb = [Sem_(ctx, nc, f"s_gb{i}") for i in range(4)]
        s_wr = [Sem_(ctx, nc, f"s_wr{i}") for i in range(4)]
        s_wtup = [Sem_(ctx, nc, f"s_wtup{i}") for i in range(4)]
        s_wtdn = [Sem_(ctx, nc, f"s_wtdn{i}") for i in range(4)]
        s_w2b = [Sem_(ctx, nc, f"s_w2b{i}") for i in range(4)]
        s_w1b = Sem_(ctx, nc, "s_w1b")
        s_ring = [Sem_(ctx, nc, f"s_ring{i}") for i in range(NRING)]
        s_qg, s_xres, s_x1s, s_x1t, s_yout = (
            Sem_(ctx, nc, n) for n in ["s_qg", "s_xres", "s_x1s", "s_x1t", "s_yout"])
        s_w1s = [Sem_(ctx, nc, f"s_w1s{i}") for i in range(4)]
        s_w2s = [Sem_(ctx, nc, f"s_w2s{i}") for i in range(4)]

        rowg = SlotGuard()

        def _stop_here():
            P.op("scalar", lambda s: s.dma_start(out=y_fm[:, :], in_=x_fm[:, :]),
                 inc=s_yout, incn=16)
            P.wait("scalar", (s_yout, s_yout.v))
            block = ctx.enter_context(nc.Block())
            P.replay(block)

        def ln_rows(stat_pe_val, n, p_sum, p_sq, p_mb, p_rb):
            P.op("vector", lambda v: v.tensor_copy(row_sum[:, 0:n], p_sum[0:1, 0:n]),
                 waits=[(sPE, stat_pe_val), rowg.use("rows_act")], inc=sDVE)
            P.op("vector", lambda v: v.tensor_copy(row_sq[:, 0:n], p_sq[0:1, 0:n]),
                 inc=sDVE)
            w_ev = sDVE.v
            P.op("vector", lambda v: v.tensor_scalar_mul(row_m[:, 0:n], row_sum[:, 0:n], 1.0 / D),
                 waits=[(sDVE, w_ev)], inc=sDVE)
            P.op("vector", lambda v: v.tensor_scalar_mul(row_t[:, 0:n], row_sq[:, 0:n], 1.0 / D),
                 inc=sDVE)
            w1 = sDVE.v
            P.op("vector", lambda v: v.tensor_mul(row_sq[:, 0:n], row_m[:, 0:n], row_m[:, 0:n]),
                 waits=[(sDVE, w1)], inc=sDVE)
            w2 = sDVE.v
            P.op("vector", lambda v: v.tensor_sub(row_t[:, 0:n], row_t[:, 0:n], row_sq[:, 0:n]),
                 waits=[(sDVE, w2)], inc=sDVE)
            wvar = sDVE.v
            P.op("scalar", lambda s: s.activation(row_r[:, 0:n], row_t[:, 0:n], AF.Ln, bias=eps_t[:]),
                 waits=[(sDVE, wvar)], inc=sACT)
            wl = sACT.v
            P.op("scalar", lambda s: s.activation(row_r[:, 0:n], row_r[:, 0:n], AF.Exp, scale=-0.5),
                 waits=[(sACT, wl)], inc=sACT)
            wr = sACT.v
            P.op("vector", lambda v: v.tensor_mul(row_m[:, 0:n], row_m[:, 0:n], row_r[:, 0:n]),
                 waits=[(sACT, wr)], inc=sDVE)
            wmr = sDVE.v
            P.op("tensor", lambda t: t.matmul(p_rb[:, 0:n], ones_f1[:], row_r[:, 0:n],
                                              start=True, stop=True),
                 waits=[(sACT, wr), (sDVE, wmr)], inc=sPE)
            P.op("tensor", lambda t: t.matmul(p_mb[:, 0:n], ones_f1[:], row_m[:, 0:n],
                                              start=True, stop=True), inc=sPE)
            rowg.set("rows_act", sACT, wr)
            return sPE.v

        p0 = ExitStack()
        p0p = ExitStack()
        x_t = [sb(f"x_t{i}", [128, 8 * 512], F32, p0) for i in range(2)]
        xbf_t2 = [sb(f"xbf_t{i}", [128, 8 * 512], BF16, p0) for i in range(2)]
        xsq_t = sb("xsq_t", [128, 512], BF16, p0)
        xn_t2 = [sb(f"xn_t{i}", [128, 8 * 512], BF16, p0) for i in range(2)]
        wq_sb = sb("wq_sb", [128, 8 * D], BF16, p0)
        wk_sb = sb("wk_sb", [128, 8 * D], BF16, p0)
        wv_sb = sb("wv_sb", [128, 8 * D], BF16, p0)
        bq_sb = sb("bq_sb", [128, 8], F32, p0)
        bk_sb = sb("bk_sb", [128, 8], F32, p0)
        bvb_sb = sb("bvb_sb", [128, D], F32, p0)
        qout = sb("qout", [128, 8 * 512], BF16, p0)
        vout = sb("vout", [128, 2 * 512], BF16, p0)
        nsq = sb("nsq", [128, 512], BF16, p0)
        kmz = sb("kmz", [1, 512], F32, p0)

        ps_sum = psm("ps_sum", [1, 512], F32, p0p)
        ps_sq = psm("ps_sq", [1, 512], F32, p0p)
        ps_mb = psm("ps_mb", [128, 512], F32, p0p)
        ps_rb = psm("ps_rb", [128, 512], F32, p0p)
        ps_p = [psm(f"ps_p{i}", [128, 512], F32, p0p) for i in range(2)]
        ps_n = psm("ps_n", [1, 512], F32, p0p)

        pguard = SlotGuard()
        bufg = SlotGuard()

        P.op("sync", lambda s: [
            s.dma_start(out=idbf_sb[:], in_=id_bf[:, :]),
            s.dma_start(out=idf_sb[:], in_=idf[:, :]),
            s.dma_start(out=bq_sb[:], in_=bq8[:, :]),
            s.dma_start(out=bk_sb[:], in_=bk8[:, :]),
            s.dma_start(out=bo_sb[:], in_=bo8[:, :]),
            s.dma_start(out=ob_sb[:], in_=bc8[:, :]),
            s.dma_start(out=bd_sb[:], in_=bd8[:, :]),
            s.dma_start(out=vcol_sb[:], in_=upb_hp[:, :]),
            s.dma_start(out=bvh[0][:], in_=bv_row[:, 0:512]),
            s.dma_start(out=bvh[1][:], in_=bv_row[:, 512:1024]),
        ], inc=s_cst, incn=16, nlist=10)
        w_consts = s_cst.v
        P.op("vector", lambda v: v.tensor_add(ob_sb[:], ob_sb[:], bd_sb[:]),
             waits=[(s_cst, w_consts)], inc=sDVE)
        w_ob = sDVE.v
        P.op("vector", lambda v: [
            v.memset(ones_bf[:], 1.0),
            v.memset(ones_f1[:], 1.0),
            v.memset(one_f[:], 1.0),
            v.memset(eps_t[:], EPS),
            v.memset(qmax_sb[:], 0.0),
            v.memset(kmax_sb[:], 0.0),
        ], inc=sDVE, incn=1, nlist=6)
        w_cdve = sDVE.v

        P.op("gpsimd", lambda g: [
            g.dma_start(out=wq_sb[:].rearrange("p (c e) -> p c e", c=8),
                        in_=wqT.ap().rearrange("(c p) e -> p c e", p=128)),
            g.dma_start(out=wk_sb[:].rearrange("p (c e) -> p c e", c=8),
                        in_=wkT.ap().rearrange("(c p) e -> p c e", p=128)),
            g.dma_start(out=wv_sb[:].rearrange("p (c e) -> p c e", c=8),
                        in_=wvT.ap().rearrange("(c p) e -> p c e", p=128)),
        ], inc=s_wc, incn=16, nlist=3)
        w_wcast = s_wc.v

        for h2 in range(2):
            P.op("tensor", lambda t, h2=h2: t.matmul(
                ps_p[h2][:, :], ones_f1[:], bvh[h2][:],
                start=True, stop=True),
                waits=[(s_cst, w_consts), (sDVE, w_cdve)], inc=sPE)
            wb = sPE.v
            P.op("vector", lambda v, h2=h2: v.tensor_copy(bvb_sb[:, h2 * 512:(h2 + 1) * 512], ps_p[h2][:, :]),
                 waits=[(sPE, wb)], inc=sDVE)
            pguard.set(h2, sDVE, sDVE.v)
        w_bvb = sDVE.v

        xload_w = {}
        xdone = {}
        for t in range(NT0):
            buf = t % 2
            xt = x_t[buf]
            xbf_t = xbf_t2[buf]
            xn_t = xn_t2[buf]
            P.op("sync", lambda s, t=t, xt=xt: s.dma_start(
                out=xt[:].rearrange("p (c f) -> p c f", c=8),
                in_=x_fm.ap().rearrange("(c p) f -> p c f", p=128)[:, :, t * 512:(t + 1) * 512]),
                waits=[bufg.use(("x", buf))], inc=s_xt[buf], incn=16)
            xload_w[t] = (s_xt[buf], s_xt[buf].v)

            P.op("vector", lambda v, xt=xt, xbf_t=xbf_t: v.tensor_copy(xbf_t[:], xt[:]),
                 waits=[xload_w[t], (sDVE, w_bvb), bufg.use(("xbf", buf))], inc=sDVE)
            w_cast = sDVE.v
            for c in range(8):
                P.op("scalar", lambda s, c=c, xt=xt: s.activation(
                    xsq_t[:], xt[:, c * 512:(c + 1) * 512], AF.Square),
                    waits=[xload_w[t], bufg.use(("xsq",))], inc=sACT)
                w_sq = sACT.v
                P.op("tensor", lambda t_, c=c, xbf_t=xbf_t: t_.matmul(
                    ps_sum[0:1, :], ones_bf[:], xbf_t[:, c * 512:(c + 1) * 512],
                    start=(c == 0), stop=(c == 7)),
                    waits=([(sDVE, w_cast), rowg.use("stat")] if c == 0 else []))
                P.op("tensor", lambda t_, c=c: t_.matmul(
                    ps_sq[0:1, :], ones_bf[:], xsq_t[:],
                    start=(c == 0), stop=(c == 7)),
                    waits=[(sACT, w_sq)], inc=sPE)
                bufg.set(("xsq",), sPE, sPE.v)
            w_stat = sPE.v
            bufg.set(("xbf", buf), sPE, w_stat)
            bc = ln_rows(w_stat, 512, ps_sum, ps_sq, ps_mb, ps_rb)
            rowg.set("stat", sDVE, sDVE.v - 5)
            for c in range(8):
                P.op("vector", lambda v, c=c, xt=xt: v.tensor_mul(
                    xt[:, c * 512:(c + 1) * 512], xt[:, c * 512:(c + 1) * 512], ps_rb[:, :]),
                    waits=([(sPE, bc), bufg.use(("xn", buf))] if c == 0 else []), inc=sDVE)
                wmu = sDVE.v
                P.op("vector", lambda v, c=c, xt=xt, xn_t=xn_t: v.tensor_sub(
                    xn_t[:, c * 512:(c + 1) * 512], xt[:, c * 512:(c + 1) * 512], ps_mb[:, :]),
                    waits=[(sDVE, wmu)], inc=sDVE)
            w_xn = sDVE.v
            bufg.set(("x", buf), sDVE, w_xn)

            for wi, (wsb, bias, outb, spill) in enumerate((
                    (wq_sb, bq_sb, qout, qspill),
                    (wk_sb, bk_sb, qout, None))):
                pmax = qmax_sb if wi == 0 else kmax_sb
                for es in range(8):
                    slot = es % 2
                    for dc in range(8):
                        P.op("tensor", lambda t_, wsb=wsb, es=es, dc=dc, slot=slot, xn_t=xn_t: t_.matmul(
                            ps_p[slot][:, :], wsb[:, dc * D + es * 128:dc * D + es * 128 + 128],
                            xn_t[:, dc * 512:(dc + 1) * 512],
                            start=(dc == 0), stop=(dc == 7)),
                            waits=([(sDVE, w_xn), (s_wc, w_wcast), pguard.use(slot)] if dc == 0 else []),
                            inc=(sPE if dc == 7 else None))
                    wmm = sPE.v
                    P.op("scalar", lambda s, es=es, slot=slot, outb=outb, bias=bias: s.activation(
                        outb[:, es * 512:(es + 1) * 512], ps_p[slot][:, :], AF.Identity,
                        bias=bias[:, es:es + 1]),
                        waits=[(sPE, wmm), bufg.use(("po", id(outb)))], inc=sACT)
                    P.op("scalar", lambda s, es=es, slot=slot, bias=bias: s.activation(
                        nsq[:], ps_p[slot][:, :], AF.Square, bias=bias[:, es:es + 1]),
                        waits=[bufg.use(("nsq",))], inc=sACT)
                    pguard.set(slot, sACT, sACT.v)
                    w_sqn = sACT.v
                    P.op("tensor", lambda t_, es=es: t_.matmul(
                        ps_n[0:1, :], ones_bf[:], nsq[:], start=(es == 0), stop=(es == 7)),
                        waits=[(sACT, w_sqn)] + ([rowg.use("psn")] if es == 0 else []),
                        inc=sPE)
                    bufg.set(("nsq",), sPE, sPE.v)
                w_nmm = sPE.v
                spill_ap = (qspill.ap() if spill is qspill else kv_cc_in.ap()[0:D, :])
                s_sp = s_qsp if spill is qspill else s_ksp
                P.op("gpsimd", lambda g, outb=outb, spill_ap=spill_ap, t=t: g.dma_start(
                    out=spill_ap.rearrange("(c p) f -> p c f", p=128)[:, :, t * 512:(t + 1) * 512],
                    in_=outb[:].rearrange("p (c f) -> p c f", c=8)),
                    waits=[(sACT, sACT.v)], inc=s_sp, incn=16)
                bufg.set(("po", id(outb)), s_sp, s_sp.v)
                P.op("vector", lambda v: v.reduce_max(tiny[:, 0:1], ps_n[0:1, :], axis=AX.X),
                     waits=[(sPE, w_nmm), rowg.use("tiny1")], inc=sDVE)
                wtm = sDVE.v
                rowg.set("psn", sDVE, wtm)
                P.op("vector", lambda v, pmax=pmax: v.tensor_max(pmax[:], pmax[:], tiny[:, 0:1]),
                     waits=[(sDVE, wtm)], inc=sDVE)
                rowg.set("tiny1", sDVE, sDVE.v)

            for ts_ in range(4):
                for eh in range(2):
                    slot = eh
                    for dc in range(8):
                        P.op("tensor", lambda t_, ts_=ts_, eh=eh, dc=dc, slot=slot, xn_t=xn_t: t_.matmul(
                            ps_p[slot][:, :], xn_t[:, dc * 512 + ts_ * 128:dc * 512 + ts_ * 128 + 128],
                            wv_sb[:, dc * D + eh * 512:dc * D + (eh + 1) * 512],
                            start=(dc == 0), stop=(dc == 7)),
                            waits=([pguard.use(slot)] if dc == 0 else []),
                            inc=(sPE if dc == 7 else None))
                    wmm = sPE.v
                    P.op("vector", lambda v, eh=eh, slot=slot: v.tensor_add(
                        vout[:, eh * 512:(eh + 1) * 512], ps_p[slot][:, :], bvb_sb[:, eh * 512:(eh + 1) * 512]),
                        waits=[(sPE, wmm), bufg.use(("vout",))], inc=sDVE)
                    pguard.set(slot, sDVE, sDVE.v)
                wvev = sDVE.v
                P.op("gpsimd", lambda g, ts_=ts_, t=t: g.dma_start(
                    out=vflat(kv_cc_in.ap(), 0, t * 512 + ts_ * 128, 128),
                    in_=vout[:]),
                    waits=[(sDVE, wvev)], inc=s_vsp, incn=16)
                bufg.set(("vout",), s_vsp, s_vsp.v)
            xdone[t] = sPE.v
            bufg.set(("xn", buf), sPE, xdone[t])
        kv_spill_w = None

        P.op("vector", lambda v: v.memset(kmz[:], 0.0), inc=sDVE)
        w_ms = sDVE.v
        P.op("vector", lambda v: v.tensor_copy(kmz[:, 0:1], kmax_sb[:]),
             waits=[(sDVE, w_ms)], inc=sDVE)
        w_km = sDVE.v
        P.op("gpsimd", lambda g: [
            g.dma_start(out=kv_cc_in[CCR - 1:CCR, j * 512:(j + 1) * 512], in_=kmz[:])
            for j in range(T // 512)],
            waits=[(sDVE, w_km)], inc=s_kmi, incn=16, nlist=T // 512)
        w_kmdma = s_kmi.v
        P.op("gpsimd", lambda g: g.collective_compute(
            "AllGather", ALU.bypass,
            replica_groups=[[0, 1, 2, 3], [4, 5, 6, 7]],
            ins=[kv_cc_in.ap().opt()],
            outs=[kv_cc_out.ap().opt()]),
            waits=[(s_ksp, s_ksp.v), (s_vsp, s_vsp.v), (s_kmi, w_kmdma)], inc=sCC)
        w_cc = sCC.v
        P.op("gpsimd", lambda g: [
            g.dma_start(out=tiny[:, r:r + 1], in_=kv_cc_out[(r + 1) * CCR - 1:(r + 1) * CCR, 0:1])
            for r in range(4)],
            waits=[(sCC, w_cc), (sDVE, w_km)], inc=s_kmo, incn=16, nlist=4)
        w_kmrb = s_kmo.v

        p0p.close()
        w_p0_dve = sDVE.v
        w_p0_act = sACT.v
        if stop_after == "p0":
            p0.close()
            P.fence(exclude=(sCC, s_kmo))
            _stop_here()
            return nc

        pp = ExitStack()
        NW = 4
        NCH = 3
        gblob = [sb(f"gblob{i}", [128, 7 * 128], F32, pp) for i in range(NW)]
        t_pp = [sb(f"t_pp{i}", [128, 128], F32, pp) for i in range(2 * NW)]
        wrow = [sb(f"wrow{i}", [128, D + 1], F32, pp) for i in range(NW)]
        wev = [sb(f"wev{i}", [128, D + 1], F32, pp) for i in range(NW)]
        wbf = [sb(f"wbf{i}", [128, D], BF16, pp) for i in range(2)]
        wtr = sb("wtr", [128, D], BF16, pp)
        ps_ch4 = [psm(f"ps_ch{i}", [128, 128], F32, pp) for i in range(NCH)]
        ps_aph = [psm(f"ps_aph{i}", [128, 512], F32, pp) for i in range(3)]
        ps_bc = psm("ps_bc", [128, NW], F32, pp)
        ps_tp = psm("ps_tp", [128, 1024], BF16, pp)

        pg = SlotGuard()
        _half = [0]
        first_prep_mm = [True]

        def prep_pass(nst, blob_dram, src_fn, ncols, sink_half, sink_bias, src_wait=None):
            for base in range(0, 32, NW):
                cs = [base + i for i in range(NW)]
                wbl = []
                for i in range(NW):
                    sgb = s_gb[i]
                    P.op("sync", lambda s, i=i, nst=nst, c=cs[i]: s.dma_start(
                        out=gblob[i][:, 0:nst * 128].rearrange("p (j f) -> p j f", j=nst),
                        in_=blob_dram.ap()[c].rearrange("j p f -> p j f")),
                        waits=[pg.use(("gblob", i))], inc=sgb, incn=16)
                    wbl.append((sgb, sgb.v))
                cur = [idf_sb] * NW
                wch = [None] * NW
                for j in range(nst):
                    for i in range(NW):
                        extra = []
                        if first_prep_mm[0]:
                            extra = [(sDVE, w_p0_dve), (sACT, w_p0_act)]
                            first_prep_mm[0] = False
                        bk = i % NCH
                        P.op("tensor", lambda t, j=j, i=i, bk=bk, cur=cur[i]: t.matmul(
                            ps_ch4[bk][:, :], gblob[i][:, j * 128:(j + 1) * 128], cur[:],
                            start=True, stop=True),
                            waits=extra + ([wbl[i], pg.use(("ps_ch", bk))]
                                           if j == 0 else [pg.use(("ps_ch", bk))]),
                            inc=sPE)
                        wpe = sPE.v
                        nxt = t_pp[i * 2 + j % 2]
                        if i % 2 == 0:
                            P.op("vector", lambda v, nxt=nxt, bk=bk: v.tensor_copy(nxt[:], ps_ch4[bk][:, :]),
                                 waits=[(sPE, wpe)], inc=sDVE)
                            pg.set(("ps_ch", bk), sDVE, sDVE.v)
                            wch[i] = (sDVE, sDVE.v)
                        else:
                            P.op("scalar", lambda s, nxt=nxt, bk=bk: s.copy(nxt[:], ps_ch4[bk][:, :]),
                                 waits=[(sPE, wpe)], inc=sACT)
                            pg.set(("ps_ch", bk), sACT, sACT.v)
                            wch[i] = (sACT, sACT.v)
                        cur[i] = nxt
                for i in range(NW):
                    pg.set(("gblob", i), sPE, sPE.v)
                wld = []
                for i in range(NW):
                    swr = s_wr[i]
                    P.op("sync", lambda s, i=i, c=cs[i]: s.dma_start(
                        out=src_fn[0](wrow[i]), in_=src_fn[1](c)),
                        waits=[pg.use(("wrow", i))] + list(src_wait or ()),
                        inc=swr, incn=16)
                    wld.append((swr, swr.v))
                ress = []
                for i in range(NW):
                    lhs = t_pp[i * 2 + (nst - 1) % 2]
                    res = {}
                    for h, (o0, o1) in enumerate(((0, 512), (512, 1024))):
                        hs = _half[0] % 3
                        _half[0] += 1
                        P.op("tensor", lambda t, i=i, lhs=lhs, o0=o0, o1=o1, hs=hs: t.matmul(
                            ps_aph[hs][:, :], lhs[:], wrow[i][:, o0:o1], start=True, stop=True),
                            waits=([wld[i], wch[i], pg.use(("aph", hs))] if h == 0
                                   else [pg.use(("aph", hs))]),
                            inc=sPE)
                        res["w%d" % h] = sPE.v
                        res["s%d" % h] = hs
                    pg.set(("wrow", i), sPE, sPE.v)
                    pg.set(("t_pp", i), sPE, sPE.v)
                    ress.append(res)
                    sink_half(cs[i], res)
                if ncols > 1024:
                    for i in range(NW):
                        lhs = t_pp[i * 2 + (nst - 1) % 2]
                        P.op("tensor", lambda t, i=i, lhs=lhs: t.matmul(
                            ps_bc[:, i:i + 1], lhs[:], wrow[i][:, 1024:1025],
                            start=True, stop=True),
                            waits=[pg.use(("bc", i))], inc=sPE)
                    wbias = sPE.v
                    for i in range(NW):
                        pg.set(("wrow", i), sPE, wbias)
                        sink_bias(cs[i], i, wbias)

        def up1_half(c, r):
            k = c % NW
            P.op("vector", lambda v, k=k, r=r: v.tensor_copy(
                wev[k][:, 0:512], ps_aph[r["s0"]][:, :]),
                waits=[(sPE, r["w0"]), pg.use(("wev", k))], inc=sDVE)
            pg.set(("aph", r["s0"]), sDVE, sDVE.v)
            P.op("vector", lambda v, k=k, r=r: v.tensor_copy(
                wev[k][:, 512:1024], ps_aph[r["s1"]][:, :]),
                waits=[(sPE, r["w1"])], inc=sDVE)
            pg.set(("aph", r["s1"]), sDVE, sDVE.v)
            pg.set(("wevh", k), sDVE, sDVE.v)

        def up1_bias(c, i, wbias):
            k = c % NW
            P.op("vector", lambda v, k=k, i=i: v.tensor_copy(
                wev[k][:, D:D + 1], ps_bc[:, i:i + 1]),
                waits=[(sPE, wbias), pg.use(("wevh", k))], inc=sDVE)
            wevd = sDVE.v
            pg.set(("bc", i), sDVE, wevd)
            P.op("scalar", lambda s, c=c, k=k: s.dma_start(
                out=wtmp_up.ap()[c * 128:(c + 1) * 128, :], in_=wev[k][:]),
                waits=[(sDVE, wevd)], inc=s_wtup[k], incn=16)
            pg.set(("wev", k), s_wtup[k], s_wtup[k].v)

        prep_pass(7, up_lo_b,
                  (lambda wr: wr[:], lambda c: wexpT_aug.ap()[c * 128:(c + 1) * 128, :]),
                  D + 1, up1_half, up1_bias)
        w_up1 = [(s, s.v) for s in s_wtup]

        def up2_half(c2, r):
            k = c2 % 2
            P.op("scalar", lambda s, k=k, r=r: s.copy(wbf[k][:, 0:512], ps_aph[r["s0"]][:, :]),
                 waits=[(sPE, r["w0"]), pg.use(("wbf", k))], inc=sACT)
            pg.set(("aph", r["s0"]), sACT, sACT.v)
            P.op("scalar", lambda s, k=k, r=r: s.copy(wbf[k][:, 512:1024], ps_aph[r["s1"]][:, :]),
                 waits=[(sPE, r["w1"])], inc=sACT)
            wevb = sACT.v
            pg.set(("aph", r["s1"]), sACT, wevb)
            for dc in range(8):
                P.op("tensor", lambda t, dc=dc, k=k: t.transpose(
                    ps_tp[:, dc * 128:(dc + 1) * 128], wbf[k][:, dc * 128:(dc + 1) * 128],
                    idbf_sb[:]),
                    waits=([(sACT, wevb), pg.use("tp")] if dc == 0 else []),
                    inc=(sPE if dc == 7 else None))
            wtp = sPE.v
            pg.set(("wbf", k), sPE, wtp)
            P.op("vector", lambda v: v.tensor_copy(wtr[:], ps_tp[:, :]),
                 waits=[(sPE, wtp), pg.use("wtr")], inc=sDVE)
            wtrd = sDVE.v
            pg.set("tp", sDVE, wtrd)
            P.op("scalar", lambda s, c2=c2: s.dma_start(
                out=w1blob.ap()[c2].rearrange("dc p f -> p dc f"),
                in_=wtr[:].rearrange("p (dc f) -> p dc f", dc=8)),
                waits=[(sDVE, wtrd)], inc=s_w1b, incn=16)
            pg.set("wtr", s_w1b, s_w1b.v)

        def up2_bias(c2, i, wbias):
            P.op("vector", lambda v, c2=c2, i=i: v.tensor_add(
                vcol_sb[:, c2:c2 + 1], ps_bc[:, i:i + 1], vcol_sb[:, c2:c2 + 1]),
                waits=[(sPE, wbias), (s_cst, w_consts), (sDVE, sDVE.v)], inc=sDVE)
            pg.set(("bc", i), sDVE, sDVE.v)

        prep_pass(5, up_hi_b,
                  (lambda wr: wr[:],
                   lambda c: wtmp_up.ap().rearrange("(p2 c2x) d -> c2x p2 d", c2x=32)[c]),
                  D + 1, up2_half, up2_bias, src_wait=w_up1)
        w_w1 = (s_w1b, s_w1b.v)

        def dna_half(c2, r):
            k = c2 % NW
            P.op("vector", lambda v, k=k, r=r: v.tensor_copy(
                wev[k][:, 0:512], ps_aph[r["s0"]][:, :]),
                waits=[(sPE, r["w0"]), pg.use(("wev", k))], inc=sDVE)
            pg.set(("aph", r["s0"]), sDVE, sDVE.v)
            P.op("vector", lambda v, k=k, r=r: v.tensor_copy(
                wev[k][:, 512:1024], ps_aph[r["s1"]][:, :]),
                waits=[(sPE, r["w1"])], inc=sDVE)
            wevd = sDVE.v
            pg.set(("aph", r["s1"]), sDVE, wevd)
            P.op("scalar", lambda s, c2=c2, k=k: s.dma_start(
                out=wtmp_dn.ap().rearrange("(p2 c2x) d -> c2x p2 d", c2x=32)[c2],
                in_=wev[k][:, 0:D]),
                waits=[(sDVE, wevd)], inc=s_wtdn[k], incn=16)
            pg.set(("wev", k), s_wtdn[k], s_wtdn[k].v)

        prep_pass(5, dn_hi_b,
                  (lambda wr: wr[:, 0:D], lambda c: wconH.ap()[c * 128:(c + 1) * 128, :]),
                  D, dna_half, None)
        w_dna = [(s, s.v) for s in s_wtdn]

        def dnb_half(c, r):
            k = c % 2
            P.op("scalar", lambda s, k=k, r=r: s.copy(wbf[k][:, 0:512], ps_aph[r["s0"]][:, :]),
                 waits=[(sPE, r["w0"]), pg.use(("wbf", k))], inc=sACT)
            pg.set(("aph", r["s0"]), sACT, sACT.v)
            P.op("scalar", lambda s, k=k, r=r: s.copy(wbf[k][:, 512:1024], ps_aph[r["s1"]][:, :]),
                 waits=[(sPE, r["w1"])], inc=sACT)
            wevb = sACT.v
            pg.set(("aph", r["s1"]), sACT, wevb)
            P.op("scalar", lambda s, c=c, k=k: [
                s.dma_start(
                    out=w2blob.ap().rearrange("(b ya) d -> ya b d", b=32)[c * 4 + a],
                    in_=wbf[k][a * 32:(a + 1) * 32, :])
                for a in range(4)],
                waits=[(sACT, wevb)], inc=s_w2b[c % NW], incn=16, nlist=4)
            pg.set(("wbf", k), s_w2b[c % NW], s_w2b[c % NW].v)

        prep_pass(7, dn_lo_b,
                  (lambda wr: wr[:, 0:D], lambda c: wtmp_dn.ap()[c * 128:(c + 1) * 128, :]),
                  D, dnb_half, None, src_wait=w_dna)
        w_w2 = [(s, s.v) for s in s_w2b]
        P.op("vector", lambda v: v.reduce_max(kmax_sb[:], tiny[:, 0:4], axis=AX.X),
             waits=[(s_kmo, w_kmrb)], inc=sDVE)
        w_rm = sDVE.v
        P.op("vector", lambda v: v.tensor_mul(kmax_sb[:], kmax_sb[:], qmax_sb[:]),
             waits=[(sDVE, w_rm)], inc=sDVE)
        w_m2 = sDVE.v
        P.op("scalar", lambda s: s.activation(kmax_sb[:], kmax_sb[:], AF.Sqrt),
             waits=[(sDVE, w_m2)], inc=sACT)
        w_m1 = sACT.v
        P.op("vector", lambda v: v.tensor_scalar_mul(kmax_sb[:], kmax_sb[:], -SCALE),
             waits=[(sACT, w_m1)], inc=sDVE)
        w_negm = sDVE.v
        P.op("tensor", lambda t: t.matmul(ps_ch4[0][:, 0:1], ones_f1[:], kmax_sb[:], start=True, stop=True),
             waits=[(sDVE, w_negm), pg.use(("ps_ch", 0))], inc=sPE)
        w_eb = sPE.v
        P.op("vector", lambda v: v.tensor_copy(expb_sb[:], ps_ch4[0][:, 0:1]),
             waits=[(sPE, w_eb)], inc=sDVE)
        w_expb = sDVE.v
        pp.close()
        p0.close()
        P.fence(exclude=(sCC, s_kmo))
        if stop_after == "prep":
            _stop_here()
            return nc

        pa = ExitStack()
        woT_sb = sb("woT_sb", [128, 8 * D], BF16, pa)
        kring = sb("kring", [128, NRING * 8 * SLOT], BF16, pa)
        vring = sb("vring", [128, NRING * 4 * D], BF16, pa)
        qgrp = sb("qgrp", [128, 8 * 512], BF16, pa)
        att = sb("att", [128, CPS * 512], BF16, pa)
        oacc = sb("oacc", [128, 4 * D], F32, pa)
        obf = sb("obf", [128, D], BF16, pa)
        ofm = sb("ofm", [128, 8 * 512], BF16, pa)
        ao = sb("ao", [128, 8 * 512], F32, pa)
        aob = sb("aob", [128, 512], BF16, pa)
        aos = sb("aos", [128, 512], BF16, pa)
        xres = sb("xres", [128, 8 * 512], F32, pa)
        linv = sb("linv", [128, 4], F32, pa)
        tsc = sb("tsc", [128, 512], F32, pa)

        ps_sc = [psm(f"ps_sc{i}", [128, 512], F32, pa) for i in range(2)]
        ps_pv = [psm(f"ps_pv{i}", [128, D], F32, pa) for i in range(2)]
        ps_l = psm("ps_l", [128, 512], F32, pa)
        ps_tr2 = psm("ps_tr2", [128, 1024], BF16, pa)
        ps_u0 = ps_pv[0]
        ps_u1 = ps_pv[1]

        P.op("gpsimd", lambda g: g.dma_start(
            out=woT_sb[:].rearrange("p (c e) -> p c e", c=8),
            in_=woT.ap().rearrange("(c p) e -> p c e", p=128)),
            inc=s_wc, incn=16)
        w_wocast = s_wc.v
        ag = SlotGuard()
        ring_ready = {}
        ring_free = {}

        def ring_load(sg):
            rb = sg % NRING
            s_abs = (sg % NSLOT) * SLOT
            rank = s_abs // T
            soff = s_abs % T
            wl = [(sCC, w_cc)]
            if sg >= NRING:
                wl.append((sPE, ring_free[sg - NRING]))
            sr = s_ring[rb]
            P.op("sync", lambda s, rb=rb, rank=rank, soff=soff: s.dma_start(
                out=kring[:, rb * 8 * SLOT:(rb + 1) * 8 * SLOT].rearrange("p (dc f) -> p dc f", dc=8),
                in_=kv_cc_out.ap()[rank * CCR:rank * CCR + D, soff:soff + SLOT]
                    .rearrange("(dc p) f -> p dc f", p=128)),
                waits=wl, inc=sr, incn=16)
            P.op("sync", lambda s, rb=rb, rank=rank, soff=soff: s.dma_start(
                out=vring[:, rb * 4 * D:(rb + 1) * 4 * D].rearrange("p (n e) -> p n e", n=4),
                in_=vflat(kv_cc_out.ap(), rank * CCR, soff, SLOT)
                    .rearrange("(n p) e -> p n e", p=128)),
                inc=sr, incn=16)
            ring_ready[sg] = (sr, sr.v)

        TOTSLOT = NG * NSLOT
        sg_next = 0
        while sg_next < min(NRING, TOTSLOT):
            ring_load(sg_next)
            sg_next += 1

        x1_store_w = {}
        for g in range(NG):
            P.op("sync", lambda s, g=g: s.dma_start(
                out=qgrp[:].rearrange("p (c f) -> p c f", c=8),
                in_=qspill.ap().rearrange("(c p) f -> p c f", p=128)[:, :, g * 512:(g + 1) * 512]),
                waits=[(s_qsp, s_qsp.v), ag.use("qgrp")], inc=s_qg, incn=16)
            wqg = (s_qg, s_qg.v)

            for sp in range(NSPAN):
                gsp = g * NSPAN + sp
                for cc_ in range(CPS):
                    sg = g * NSLOT + sp * SPS + cc_ // 4
                    rb = sg % NRING
                    sc = cc_ % 4
                    slot = cc_ % 2
                    wl = [ring_ready[sg]]
                    if cc_ == 0:
                        wl += [wqg, (sDVE, w_expb)]
                    wl.append(ag.use(("sc", slot)))
                    for dc in range(8):
                        P.op("tensor", lambda t, rb=rb, sc=sc, dc=dc, slot=slot: t.matmul(
                            ps_sc[slot][:, :],
                            kring[:, rb * 8 * SLOT + dc * SLOT + sc * 128:
                                   rb * 8 * SLOT + dc * SLOT + sc * 128 + 128],
                            qgrp[:, dc * 512:(dc + 1) * 512],
                            start=(dc == 0), stop=(dc == 7)),
                            waits=(wl if dc == 0 else []),
                            inc=(sPE if dc == 7 else None))
                    wsc_ = sPE.v
                    P.op("scalar", lambda s, cc_=cc_, slot=slot: s.activation(
                        att[:, cc_ * 512:(cc_ + 1) * 512], ps_sc[slot][:, :], AF.Exp,
                        bias=expb_sb[:], scale=SCALE),
                        waits=[(sPE, wsc_)], inc=sACT)
                    ag.set(("sc", slot), sACT, sACT.v)
                w_exps = sACT.v
                for cc_ in range(CPS):
                    P.op("tensor", lambda t, cc_=cc_, sp=sp: t.matmul(
                        ps_l[0:1, :], ones_bf[:], att[:, cc_ * 512:(cc_ + 1) * 512],
                        start=(sp == 0 and cc_ == 0), stop=(sp == NSPAN - 1 and cc_ == CPS - 1)),
                        waits=([(sACT, w_exps), ag.use("psl")] if cc_ == 0 else []),
                        inc=(sPE if cc_ == CPS - 1 else None))
                for qt in range(4):
                    slot = qt % 2
                    wl = [(sACT, w_exps), ag.use(("pv", slot))]
                    for cc_ in range(CPS):
                        sg = g * NSLOT + sp * SPS + cc_ // 4
                        rb = sg % NRING
                        for eh in range(2):
                            P.op("tensor", lambda t, qt=qt, cc_=cc_, eh=eh, slot=slot, rb=rb: t.matmul(
                                ps_pv[slot][:, eh * 512:(eh + 1) * 512],
                                att[:, cc_ * 512 + qt * 128: cc_ * 512 + qt * 128 + 128],
                                vring[:, rb * 4 * D + (cc_ % 4) * D + eh * 512:
                                       rb * 4 * D + (cc_ % 4) * D + (eh + 1) * 512],
                                start=(cc_ == 0), stop=(cc_ == CPS - 1)),
                                waits=(wl if (cc_ == 0 and eh == 0) else []),
                                inc=(sPE if (cc_ == CPS - 1 and eh == 1) else None))
                    wpv = sPE.v
                    if sp == 0:
                        P.op("vector", lambda v, qt=qt, slot=slot: [
                            v.tensor_copy(oacc[:, qt * D:qt * D + 512], ps_pv[slot][:, 0:512]),
                            v.tensor_copy(oacc[:, qt * D + 512:(qt + 1) * D], ps_pv[slot][:, 512:D]),
                        ], waits=[(sPE, wpv), ag.use(("oacc", qt))], inc=sDVE, incn=1, nlist=2)
                    else:
                        P.op("vector", lambda v, qt=qt, slot=slot: [
                            v.tensor_add(oacc[:, qt * D:qt * D + 512],
                                         oacc[:, qt * D:qt * D + 512], ps_pv[slot][:, 0:512]),
                            v.tensor_add(oacc[:, qt * D + 512:(qt + 1) * D],
                                         oacc[:, qt * D + 512:(qt + 1) * D], ps_pv[slot][:, 512:D]),
                        ], waits=[(sPE, wpv)], inc=sDVE, incn=1, nlist=2)
                    ag.set(("pv", slot), sDVE, sDVE.v)
                w_pvspan = sPE.v
                for j in range(SPS):
                    ring_free[g * NSLOT + sp * SPS + j] = w_pvspan
                while sg_next < min(TOTSLOT, g * NSLOT + (sp + 1) * SPS + NRING):
                    if sg_next - NRING >= 0 and sg_next - NRING not in ring_free:
                        break
                    ring_load(sg_next)
                    sg_next += 1
            w_lastflush = sDVE.v
            ag.set("qgrp", sPE, sPE.v)

            P.op("vector", lambda v: v.tensor_copy(row_t[:, :], ps_l[0:1, :]),
                 waits=[(sPE, sPE.v), rowg.use("rows_act")], inc=sDVE)
            w_lrow = sDVE.v
            for qt in range(4):
                P.op("tensor", lambda t, qt=qt: t.matmul(
                    ps_l[0:128, qt:qt + 1], row_t[0:1, qt * 128:(qt + 1) * 128], one_f[:],
                    start=(qt == 0), stop=(qt == 3)),
                    waits=([(sDVE, w_lrow)] if qt == 0 else []),
                    inc=(sPE if qt == 3 else None))
            w_lcols = sPE.v
            P.op("vector", lambda v: v.reciprocal(linv[:, :], ps_l[0:128, 0:4]),
                 waits=[(sPE, w_lcols)], inc=sDVE)
            w_linv = sDVE.v
            ag.set("psl", sDVE, w_linv)

            for qt in range(4):
                P.op("vector", lambda v, qt=qt: v.tensor_scalar(
                    obf[:], oacc[:, qt * D:(qt + 1) * D], linv[:, qt:qt + 1], None, ALU.mult),
                    waits=[(sDVE, w_linv), (sDVE, w_lastflush), ag.use("obf")],
                    inc=sDVE)
                w_obf = sDVE.v
                ag.set(("oacc", qt), sDVE, w_obf)
                for ec in range(8):
                    P.op("tensor", lambda t, ec=ec: t.transpose(
                        ps_tr2[:, ec * 128:(ec + 1) * 128], obf[:, ec * 128:(ec + 1) * 128],
                        idbf_sb[:]),
                        waits=([(sDVE, w_obf), ag.use("tr")] if ec == 0 else []),
                        inc=(sPE if ec == 7 else None))
                wtp = sPE.v
                P.op("vector", lambda v, qt=qt: v.tensor_copy(
                    ofm[:].rearrange("p (ec q) -> p ec q", ec=8)[:, :, qt * 128:(qt + 1) * 128],
                    ps_tr2[:, :].rearrange("p (ec q) -> p ec q", ec=8)),
                    waits=[(sPE, wtp), ag.use("ofm")], inc=sDVE)
                ag.set("tr", sDVE, sDVE.v)
                ag.set("obf", sPE, sPE.v)
            w_ofm = sDVE.v

            P.op("sync", lambda s, g=g: s.dma_start(
                out=xres[:].rearrange("p (c f) -> p c f", c=8),
                in_=x_fm.ap().rearrange("(c p) f -> p c f", p=128)[:, :, g * 512:(g + 1) * 512]),
                waits=[ag.use("xres")], inc=s_xres, incn=16)
            w_xres = (s_xres, s_xres.v)

            for es in range(8):
                slot = es % 2
                for ec in range(8):
                    P.op("tensor", lambda t, es=es, ec=ec, slot=slot: t.matmul(
                        ps_sc[slot][:, :], woT_sb[:, ec * D + es * 128:ec * D + es * 128 + 128],
                        ofm[:, ec * 512:(ec + 1) * 512],
                        start=(ec == 0), stop=(ec == 7)),
                        waits=([(sDVE, w_ofm), (s_wc, w_wocast), ag.use(("sc", slot))] if ec == 0 else []),
                        inc=(sPE if ec == 7 else None))
                w_aomm = sPE.v
                P.op("scalar", lambda s, es=es, slot=slot: s.activation(
                    ao[:, es * 512:(es + 1) * 512], ps_sc[slot][:, :], AF.Identity,
                    bias=bo_sb[:, es:es + 1]),
                    waits=[(sPE, w_aomm), ag.use("ao")], inc=sACT)
                P.op("scalar", lambda s, es=es, slot=slot: s.activation(
                    aob[:], ps_sc[slot][:, :], AF.Identity, bias=bo_sb[:, es:es + 1]),
                    waits=[ag.use("aob")], inc=sACT)
                w_aob = sACT.v
                P.op("scalar", lambda s, es=es, slot=slot: s.activation(
                    aos[:], ps_sc[slot][:, :], AF.Square, bias=bo_sb[:, es:es + 1]),
                    waits=[ag.use("aos")], inc=sACT)
                ag.set(("sc", slot), sACT, sACT.v)
                w_aos = sACT.v
                P.op("tensor", lambda t, es=es: t.matmul(
                    ps_u0[0:1, 0:512], ones_bf[:], aob[:], start=(es == 0), stop=(es == 7)),
                    waits=[(sACT, w_aob)] + ([(sDVE, w_lastflush), ag.use(("pv", 0))] if es == 0 else []),
                    inc=sPE)
                ag.set("aob", sPE, sPE.v)
                P.op("tensor", lambda t, es=es: t.matmul(
                    ps_u1[0:1, 0:512], ones_bf[:], aos[:], start=(es == 0), stop=(es == 7)),
                    waits=[(sACT, w_aos)] + ([ag.use(("pv", 1))] if es == 0 else []), inc=sPE)
                ag.set("aos", sPE, sPE.v)
            w_stato = sPE.v
            bc1o = ln_rows(w_stato, 512, ps_u0, ps_u1, ps_sc[0], ps_sc[1])
            ag.set(("pv", 0), sDVE, sDVE.v - 5)
            ag.set(("pv", 1), sDVE, sDVE.v - 5)
            for c in range(8):
                P.op("vector", lambda v, c=c: v.tensor_mul(
                    tsc[:], ao[:, c * 512:(c + 1) * 512], ps_sc[1][:, :]),
                    waits=([(sPE, bc1o), w_xres] if c == 0 else [(sDVE, sDVE.v)]),
                    inc=sDVE)
                w1_ = sDVE.v
                P.op("vector", lambda v, c=c: v.tensor_sub(
                    ao[:, c * 512:(c + 1) * 512], tsc[:], ps_sc[0][:, :]),
                    waits=[(sDVE, w1_)], inc=sDVE)
                w2_ = sDVE.v
                P.op("vector", lambda v, c=c: v.tensor_add(
                    ao[:, c * 512:(c + 1) * 512], ao[:, c * 512:(c + 1) * 512],
                    xres[:, c * 512:(c + 1) * 512]),
                    waits=[(sDVE, w2_)], inc=sDVE)
            w_x1 = sDVE.v
            ag.set("xres", sDVE, w_x1)
            ag.set(("sc", 0), sDVE, w_x1)
            ag.set(("sc", 1), sDVE, w_x1)
            ag.set("ao", sDVE, w_x1)
            P.op("scalar", lambda s, g=g: s.dma_start(
                out=x1sp.ap().rearrange("(c p) f -> p c f", p=128)[:, :, g * 512:(g + 1) * 512],
                in_=ao[:].rearrange("p (c f) -> p c f", c=8)),
                waits=[(sDVE, w_x1)], inc=s_x1s, incn=16)
            x1_store_w[g] = (s_x1s, s_x1s.v)
            ag.set("ao", s_x1s, s_x1s.v)
            ag.set("ofm", sPE, w_stato)

        pa.close()
        P.fence()
        if stop_after == "attn":
            _stop_here()
            return nc

        pf = ExitStack()
        x1t2 = [sb(f"x1t{i}", [128, 8 * 512], F32, pf) for i in range(2)]
        x1bf = sb("x1bf", [128, 512], BF16, pf)
        x1sq = sb("x1sq", [128, 512], BF16, pf)
        xn2 = sb("xn2", [128, 8 * 512], BF16, pf)
        w1s = [sb(f"w1s{i}", [128, 8 * 128], BF16, pf) for i in range(4)]
        w2s = [sb(f"w2s{i}", [128, D], BF16, pf) for i in range(4)]
        gt = sb("gt", [128, 32 * 512], BF16, pf)
        yv = sb("yv", [128, 8 * 512], F32, pf)
        yb = sb("yb", [128, 512], BF16, pf)
        ysq = sb("ysq", [128, 512], BF16, pf)
        tf = sb("tf", [128, 512], F32, pf)
        tfg = sb("tfg", [128, 512], F32, pf)
        fbm_sb = sb("fbm_sb", [128, 512], F32, pf)
        fbr_sb = sb("fbr_sb", [128, 512], F32, pf)

        ps_f = [psm(f"ps_f{i}", [128, 512], F32, pf) for i in range(2)]
        ps_fb = [psm(f"ps_fb{i}", [128, 512], F32, pf) for i in range(2)]
        ps_dn = [psm(f"ps_dn{i}", [128, 512], F32, pf) for i in range(4)]

        fg = SlotGuard()
        for ft in range(NFT):
            x1t = x1t2[ft % 2]
            P.op("sync", lambda s, ft=ft, x1t=x1t: s.dma_start(
                out=x1t[:].rearrange("p (c f) -> p c f", c=8),
                in_=x1sp.ap().rearrange("(c p) f -> p c f", p=128)[:, :, ft * 512:(ft + 1) * 512]),
                waits=[x1_store_w[ft], fg.use(("x1t", ft % 2))], inc=s_x1t, incn=16)
            w_x1t = (s_x1t, s_x1t.v)
            for c in range(8):
                P.op("vector", lambda v, c=c, x1t=x1t: v.tensor_copy(x1bf[:], x1t[:, c * 512:(c + 1) * 512]),
                     waits=[w_x1t, fg.use("x1bf")], inc=sDVE)
                w_cb = sDVE.v
                P.op("scalar", lambda s, c=c, x1t=x1t: s.activation(x1sq[:], x1t[:, c * 512:(c + 1) * 512], AF.Square),
                     waits=[w_x1t, fg.use("x1sq")], inc=sACT)
                w_sb2 = sACT.v
                P.op("tensor", lambda t, c=c: t.matmul(
                    ps_f[0][0:1, :], ones_bf[:], x1bf[:], start=(c == 0), stop=(c == 7)),
                    waits=[(sDVE, w_cb)] + ([fg.use("psf")] if c == 0 else []),
                    inc=sPE)
                fg.set("x1bf", sPE, sPE.v)
                P.op("tensor", lambda t, c=c: t.matmul(
                    ps_f[1][0:1, :], ones_bf[:], x1sq[:], start=(c == 0), stop=(c == 7)),
                    waits=[(sACT, w_sb2)], inc=sPE)
                fg.set("x1sq", sPE, sPE.v)
            bcf = ln_rows(sPE.v, 512, ps_f[0], ps_f[1], ps_fb[0], ps_fb[1])
            fg.set("psf", sDVE, sDVE.v - 5)
            for c in range(8):
                P.op("vector", lambda v, c=c, x1t=x1t: v.tensor_mul(
                    tf[:], x1t[:, c * 512:(c + 1) * 512], ps_fb[1][:, :]),
                    waits=([(sPE, bcf), fg.use("tf")] if c == 0 else [(sDVE, sDVE.v)]),
                    inc=sDVE)
                wmu = sDVE.v
                P.op("vector", lambda v, c=c: v.tensor_sub(
                    xn2[:, c * 512:(c + 1) * 512], tf[:], ps_fb[0][:, :]),
                    waits=[(sDVE, wmu)] + ([fg.use("xn2")] if c == 0 else []), inc=sDVE)
            w_xn2 = sDVE.v
            fg.set("tf", sDVE, w_xn2)

            for hc in range(32):
                wb = w1s[hc % 4]
                P.op("sync", lambda s, hc=hc, wb=wb: s.dma_start(
                    out=wb[:].rearrange("p (dc f) -> p dc f", dc=8),
                    in_=w1blob.ap()[hc].rearrange("dc p f -> p dc f")),
                    waits=[w_w1, fg.use(("w1", hc % 4))], inc=s_w1s[hc % 4], incn=16)
                wld = (s_w1s[hc % 4], s_w1s[hc % 4].v)
                slot = hc % 2
                for dc in range(8):
                    P.op("tensor", lambda t, hc=hc, dc=dc, wb=wb, slot=slot: t.matmul(
                        ps_f[slot][:, :], wb[:, dc * 128:(dc + 1) * 128],
                        xn2[:, dc * 512:(dc + 1) * 512],
                        start=(dc == 0), stop=(dc == 7)),
                        waits=([(sDVE, w_xn2), wld, fg.use(("psf2", slot))] if dc == 0 else []),
                        inc=(sPE if dc == 7 else None))
                fg.set(("w1", hc % 4), sPE, sPE.v)
                wmm = sPE.v
                P.op("scalar", lambda s, hc=hc, slot=slot: s.activation(
                    gt[:, hc * 512:(hc + 1) * 512], ps_f[slot][:, :], AF.Gelu,
                    bias=vcol_sb[:, hc:hc + 1]),
                    waits=[(sPE, wmm)] + ([fg.use("gt")] if hc == 0 else []), inc=sACT)
                fg.set(("psf2", slot), sACT, sACT.v)
            w_gt = sACT.v
            fg.set("psf", sACT, w_gt)

            for half in range(2):
                for hc in range(32):
                    wb = w2s[hc % 4]
                    P.op("sync", lambda s, hc=hc, wb=wb, half=half: s.dma_start(
                        out=wb[:, half * 512:(half + 1) * 512],
                        in_=w2blob.ap()[hc * 128:(hc + 1) * 128, half * 512:(half + 1) * 512]),
                        waits=list(w_w2) + [fg.use(("w2", hc % 4))], inc=s_w2s[hc % 4], incn=16)
                    wld = (s_w2s[hc % 4], s_w2s[hc % 4].v)
                    for ds in range(4):
                        P.op("tensor", lambda t, hc=hc, ds=ds, wb=wb, half=half: t.matmul(
                            ps_dn[ds][:, :], wb[:, half * 512 + ds * 128: half * 512 + (ds + 1) * 128],
                            gt[:, hc * 512:(hc + 1) * 512],
                            start=(hc == 0), stop=(hc == 31)),
                            waits=(([(sACT, w_gt)] if hc == 0 and ds == 0 and half == 0 else []) +
                                   ([wld] if ds == 0 else []) +
                                   ([fg.use(("psdn", ds))] if hc == 0 else [])),
                            inc=(sPE if ds == 3 else None))
                    fg.set(("w2", hc % 4), sPE, sPE.v)
                w_dn = sPE.v
                for ds in range(4):
                    es = half * 4 + ds
                    P.op("scalar", lambda s, ds=ds, es=es: s.activation(
                        yv[:, es * 512:(es + 1) * 512], ps_dn[ds][:, :], AF.Identity,
                        bias=ob_sb[:, es:es + 1]),
                        waits=[(sPE, w_dn), (sDVE, w_ob)] + ([fg.use("yv")] if es == 0 else []),
                        inc=sACT)
                    P.op("scalar", lambda s, ds=ds, es=es: s.activation(
                        yb[:], ps_dn[ds][:, :], AF.Identity, bias=ob_sb[:, es:es + 1]),
                        waits=[fg.use("yb")], inc=sACT)
                    w_yb = sACT.v
                    P.op("scalar", lambda s, ds=ds, es=es: s.activation(
                        ysq[:], ps_dn[ds][:, :], AF.Square, bias=ob_sb[:, es:es + 1]),
                        waits=[fg.use("ysq")], inc=sACT)
                    w_ysq = sACT.v
                    fg.set(("psdn", ds), sACT, sACT.v)
                    P.op("tensor", lambda t, es=es: t.matmul(
                        ps_f[0][0:1, :], ones_bf[:], yb[:], start=(es == 0), stop=(es == 7)),
                        waits=[(sACT, w_yb)] + ([fg.use("psf")] if es == 0 else []),
                        inc=sPE)
                    fg.set("yb", sPE, sPE.v)
                    P.op("tensor", lambda t, es=es: t.matmul(
                        ps_f[1][0:1, :], ones_bf[:], ysq[:], start=(es == 0), stop=(es == 7)),
                        waits=[(sACT, w_ysq)], inc=sPE)
                    fg.set("ysq", sPE, sPE.v)
            fg.set("gt", sPE, sPE.v)
            bcf2 = ln_rows(sPE.v, 512, ps_f[0], ps_f[1], ps_fb[0], ps_fb[1])
            fg.set("psf", sDVE, sDVE.v - 5)
            P.op("vector", lambda v: v.tensor_copy(fbm_sb[:], ps_fb[0][:, :]),
                 waits=[(sPE, bcf2), fg.use("fbm")], inc=sDVE)
            P.op("vector", lambda v: v.tensor_copy(fbr_sb[:], ps_fb[1][:, :]),
                 inc=sDVE)
            w_fbc = sDVE.v
            for c in range(6):
                P.op("vector", lambda v, c=c: v.tensor_mul(
                    tf[:], yv[:, c * 512:(c + 1) * 512], ps_fb[1][:, :]),
                    waits=([fg.use("tf")] if c == 0 else [(sDVE, sDVE.v)]),
                    inc=sDVE)
                w1_ = sDVE.v
                P.op("vector", lambda v, c=c: v.tensor_sub(
                    yv[:, c * 512:(c + 1) * 512], tf[:], ps_fb[0][:, :]),
                    waits=[(sDVE, w1_)], inc=sDVE)
                w2_ = sDVE.v
                P.op("vector", lambda v, c=c, x1t=x1t: v.tensor_add(
                    yv[:, c * 512:(c + 1) * 512], yv[:, c * 512:(c + 1) * 512],
                    x1t[:, c * 512:(c + 1) * 512]),
                    waits=[(sDVE, w2_)], inc=sDVE)
            w_y = sDVE.v
            for c in range(6, 8):
                P.op("gpsimd", lambda gp, c=c: gp.tensor_mul(
                    tfg[:], yv[:, c * 512:(c + 1) * 512], fbr_sb[:]),
                    waits=([(sDVE, w_fbc), fg.use("tfg")] if c == 6 else [(sGP, sGP.v)]),
                    inc=sGP)
                g1_ = sGP.v
                P.op("gpsimd", lambda gp, c=c: gp.tensor_sub(
                    yv[:, c * 512:(c + 1) * 512], tfg[:], fbm_sb[:]),
                    waits=[(sGP, g1_)], inc=sGP)
                g2_ = sGP.v
                P.op("gpsimd", lambda gp, c=c, x1t=x1t: gp.tensor_add(
                    yv[:, c * 512:(c + 1) * 512], yv[:, c * 512:(c + 1) * 512],
                    x1t[:, c * 512:(c + 1) * 512]),
                    waits=[(sGP, g2_)], inc=sGP)
            w_yg = sGP.v
            fg.set("tf", sDVE, w_y)
            fg.set("tfg", sGP, w_yg)
            fg.set("fbm", sGP, w_yg)
            P.op("scalar", lambda s, ft=ft: s.dma_start(
                out=y_fm.ap().rearrange("(c p) f -> p c f", p=128)[:, :, ft * 512:(ft + 1) * 512],
                in_=yv[:].rearrange("p (c f) -> p c f", c=8)),
                waits=[(sDVE, w_y), (sGP, w_yg)], inc=s_yout, incn=16)
            fg.set(("x1t", ft % 2), s_yout, s_yout.v)
            fg.set("yv", s_yout, s_yout.v)
        pf.close()

        P.wait("scalar", (s_yout, s_yout.v))

        block = ctx.enter_context(nc.Block())
        P.replay(block)

    return nc


def _prep_inputs(L0_attn_x, L0_attn_w_q, L0_attn_b_q, L0_attn_w_k, L0_attn_b_k,
                 L0_attn_w_v, L0_attn_b_v, L0_attn_w_out, L0_attn_b_out,
                 L0_ffn_W_exp, L0_ffn_b_exp, L0_ffn_up_weights, L0_ffn_up_bias,
                 L0_ffn_down_weights, L0_ffn_W_con, L0_ffn_b_con, L0_ffn_down_bias,
                 T=4096):
    x = np.asarray(L0_attn_x, np.float32).reshape(-1, D)
    up_lo, up_hi, dn_hi, dn_lo = _host_blobs(
        np.asarray(L0_ffn_up_weights, np.float32),
        np.asarray(L0_ffn_down_weights, np.float32))
    wexpT_aug = np.ascontiguousarray(
        np.concatenate([np.asarray(L0_ffn_W_exp, np.float32).T,
                        np.asarray(L0_ffn_b_exp, np.float32)[:, None]], axis=1))
    idx = np.arange(H)
    hp_rows = (idx % 128) * 32 + idx // 128
    wconH = np.ascontiguousarray(np.asarray(L0_ffn_W_con, np.float32)[hp_rows])
    shared = dict(
        wqT=np.ascontiguousarray(np.asarray(L0_attn_w_q, np.float32).T),
        wkT=np.ascontiguousarray(np.asarray(L0_attn_w_k, np.float32).T),
        wvT=np.ascontiguousarray(np.asarray(L0_attn_w_v, np.float32).T),
        woT=np.ascontiguousarray(np.asarray(L0_attn_w_out, np.float32).T),
        bq8=_col8(L0_attn_b_q), bk8=_col8(L0_attn_b_k), bo8=_col8(L0_attn_b_out),
        bv_row=np.ascontiguousarray(np.asarray(L0_attn_b_v, np.float32)[None, :]),
        id_bf=np.eye(128, dtype=BF),
        idf=np.eye(128, dtype=np.float32),
        wexpT_aug=wexpT_aug, wconH=wconH,
        up_lo_b=up_lo, up_hi_b=up_hi, dn_hi_b=dn_hi, dn_lo_b=dn_lo,
        upb_hp=np.ascontiguousarray(np.asarray(L0_ffn_up_bias, np.float32).reshape(128, 32)),
        bc8=_col8(L0_ffn_b_con), bd8=_col8(L0_ffn_down_bias),
    )
    in_maps = []
    n = x.shape[0] // T
    for c in range(n):
        m = dict(shared)
        m["x_fm"] = np.ascontiguousarray(x[c * T:(c + 1) * T].T)
        in_maps.append(m)
    return in_maps


def kernel(**inputs):
    T = 4096
    in_maps = _prep_inputs(**inputs, T=T)
    nc = build_nc(T)
    res = run_bass_kernel_spmd(nc, in_maps, core_ids=list(range(NCORES)))
    cols = [res.results[c]["y_fm"] for c in range(NCORES)]
    full = np.concatenate(cols, axis=1)
    return np.ascontiguousarray(full.T).reshape(2, 16384, D).astype(np.float32)
